# version 04
# baseline (speedup 1.0000x reference)
from contextlib import ExitStack

import numpy as np
import ml_dtypes

import concourse.bass as bass
import concourse.mybir as mybir
from concourse.bass_utils import run_bass_kernel_spmd

F32 = mybir.dt.float32
BF16 = mybir.dt.bfloat16
AF = mybir.ActivationFunctionType
ALU = mybir.AluOpType
AX = mybir.AxisListType
BF = ml_dtypes.bfloat16

S, D, NH, NG, HD = 8192, 2048, 32, 8, 64
NT, TPC = 64, 8
NE, I_EXP, I_SH = 16, 1408, 5632
NCORE = 8
DEBUG = 0


def core_tiles(c):
    return [c, 15 - c, 16 + c, 31 - c, 32 + c, 47 - c, 48 + c, 63 - c]


def _bf(x):
    return np.asarray(x, dtype=BF)



def build_constants():
    con = {}
    inv = 1.0 / 10000.0 ** (np.arange(0, HD, 2, dtype=np.float64) / HD)
    fr = np.arange(S, dtype=np.float64)[:, None] * inv[None, :]
    con["cosn"] = np.repeat(np.cos(fr), 2, axis=1).astype(np.float32)
    con["sinn"] = np.repeat(np.sin(fr), 2, axis=1).astype(np.float32)
    slopes = 2.0 ** (-(np.arange(1, NH + 1, dtype=np.float64)) * 8.0 / NH)
    s_hi = _bf(slopes).astype(np.float64)
    s_lo = _bf(slopes - s_hi).astype(np.float64)
    con["slope_hi"], con["slope_lo"], con["slope_eff"] = s_hi, s_lo, s_hi + s_lo
    t = np.arange(S, dtype=np.float64).reshape(NT, 128)
    t_hi = _bf(t).astype(np.float64)
    t_lo = t - t_hi
    augk = np.zeros((NT, 5, 8 * 128), np.float64)
    for g in range(8):
        sl = slice(g * 128, g * 128 + 128)
        augk[:, 0, sl] = t_hi
        augk[:, 1, sl] = t_lo
        augk[:, 2, sl] = t_hi
        augk[:, 3, sl] = 1.0
        augk[:, 4, sl] = 1.0
    con["augk"] = _bf(augk)
    kk = np.arange(128)
    con["mask0"] = _bf((kk[None, :] >= kk[:, None]).astype(np.float32))
    con["idn"] = np.eye(128, dtype=BF)
    return con


def build_augq(c, con):
    tiles = core_tiles(c)
    aq = np.zeros((5, NH * 1024), np.float64)
    for h in range(NH):
        for j, qt in enumerate(tiles):
            sl = slice(h * 1024 + j * 128, h * 1024 + j * 128 + 128)
            s_glob = qt * 128 + np.arange(128, dtype=np.float64)
            X = -con["slope_eff"][h] * s_glob
            X_hi = _bf(X).astype(np.float64)
            aq[0, sl] = con["slope_hi"][h]
            aq[1, sl] = con["slope_hi"][h]
            aq[2, sl] = con["slope_lo"][h]
            aq[3, sl] = X_hi
            aq[4, sl] = X - X_hi
    return _bf(aq)



class Prog:
    def __init__(self, nc, stack):
        self.nc = nc
        self.sem = {}
        for name in ["sdX", "sdW", "sPE", "sACT", "sDVE", "sVT"]:
            self.sem[name] = [stack.enter_context(nc.semaphore(name)), 0]

    def inc(self, name, inst, k=1):
        s = self.sem[name]
        inst.then_inc(s[0], k)
        s[1] += k
        return s[1]

    def n(self, name):
        return self.sem[name][1]

    def wait(self, name, eng, at=None):
        s = self.sem[name]
        eng.wait_ge(s[0], s[1] if at is None else at)

    def vt(self, inst):
        self.inc("sVT", inst)
        self.nc.vector.wait_ge(self.sem["sVT"][0], self.sem["sVT"][1])
        return self.sem["sVT"][1]

    def snapshot(self):
        return {k: v[1] for k, v in self.sem.items()}

    def restore(self, snap):
        for k, v in snap.items():
            self.sem[k][1] = v


def build_nc(ph=7):
    nc = bass.Bass("TRN2", target_bir_lowering=False, debug=False,
                   detect_race_conditions=False)

    def din(name, shape, dtype=F32):
        return nc.dram_tensor(name, list(shape), dtype, kind="ExternalInput")

    x_h = din("x", (S, D))
    xo_h = din("x_own", (TPC * 128, D))
    wq_h = din("wq_s", (4, 128, 16 * 512))
    wk_h = din("wk_s", (128, 16 * 512))
    wv_h = din("wv_s", (128, 16 * 512))
    wo_h = din("wo_s", (128, 16 * 2048))
    bo_h = din("bo_b", (128, D))
    wr_h = din("wr_s", (128, 16 * 16))
    wg_h = din("wg_s", (NE, 11, 128, 16 * 128))
    wu_h = din("wu_s", (NE, 11, 128, 16 * 128))
    wd_h = din("wd_s", (NE, 4, 11, 128, 512))
    wsg_h = din("wsg_s", (44, 128, 16 * 128))
    wsu_h = din("wsu_s", (44, 128, 16 * 128))
    wsd_h = din("wsd_s", (4, 44, 128, 512))
    wshg_h = din("wshg_s", (128, 16))
    cosn_h = din("cosn", (S, HD))
    sinn_h = din("sinn", (S, HD))
    coso_h = din("cos_own", (TPC * 128, HD))
    sino_h = din("sin_own", (TPC * 128, HD))
    augq_h = din("augq", (5, NH * 1024), BF16)
    augk_h = din("augk", (NT, 5, 1024), BF16)
    mask_h = din("mask0", (128, 128), BF16)
    idn_h = din("idn", (128, 128), BF16)
    cst_h = din("cst", (128, 8))

    out_h = nc.dram_tensor("out_shard", [TPC * 128, D], F32, kind="ExternalOutput")
    kscr_h = nc.dram_tensor("kscr", [NT, 128, 1544], BF16, kind="Internal")
    if DEBUG:
        dbg_xn = nc.dram_tensor("dbg_xn", [NT, 128, D], BF16, kind="Internal")
        dbg_kn = nc.dram_tensor("dbg_kn", [NT, 128, 512], BF16, kind="Internal")
        dbg_kp = nc.dram_tensor("dbg_kp", [NT, 128, 512], F32, kind="Internal")
        dbg_qt = nc.dram_tensor("dbg_qt", [69, NH * 1024], BF16, kind="Internal")
        dbg_ra = nc.dram_tensor("dbg_ra", [128, TPC * D], F32, kind="Internal")
        dbg_wol = nc.dram_tensor("dbg_wol", [128, TPC * 2048], BF16, kind="Internal")
        dbg_ao = nc.dram_tensor("dbg_ao", [TPC, 128, D], F32, kind="Internal")
        dbg_es = nc.dram_tensor("dbg_es", [4, 128, 1024], BF16, kind="Internal")
        dbg_cx = nc.dram_tensor("dbg_cx", [4, 65, 1024], F32, kind="Internal")
        dbg_e2 = nc.dram_tensor("dbg_e2", [16, 128, 1024], BF16, kind="Internal")
        dbg_c2 = nc.dram_tensor("dbg_c2", [65, 1024], F32, kind="Internal")
        dbg_c3 = nc.dram_tensor("dbg_c3", [16, 65, 512], F32, kind="Internal")
        dbg_lg = nc.dram_tensor("dbg_lg", [TPC, 128, 17], F32, kind="Internal")
        dbg_x2 = nc.dram_tensor("dbg_x2", [TPC, 128, D], BF16, kind="Internal")
        dbg_ya = nc.dram_tensor("dbg_ya", [128, TPC * D], F32, kind="Internal")
        dbg_at = nc.dram_tensor("dbg_at", [128, 11 * 1024], BF16, kind="Internal")
        dbg_yp = nc.dram_tensor("dbg_yp", [20, 128, TPC * D], F32, kind="Internal")

    stack = ExitStack()
    e = stack.enter_context

    resid = e(nc.sbuf_tensor("resid", [128, TPC * D], F32))
    cst = e(nc.sbuf_tensor("cst_s", [128, 8], F32))
    idn = e(nc.sbuf_tensor("idn_s", [128, 128], BF16))
    mask0 = e(nc.sbuf_tensor("mask0_s", [128, 128], BF16))
    ones = e(nc.sbuf_tensor("ones_s", [1, 128], F32))
    ctabK = e(nc.sbuf_tensor("ctabK", [128, HD], F32))
    stabK = e(nc.sbuf_tensor("stabK", [128, HD], F32))
    comb = e(nc.sbuf_tensor("comb_s", [128, TPC * NE], F32))
    sgt = e(nc.sbuf_tensor("sg_s", [128, TPC], F32))
    rsb = e(nc.sbuf_tensor("rsb", [1, 128], F32))
    rbsb = e(nc.sbuf_tensor("rbsb", [64, 128], F32))
    rtmp = e(nc.sbuf_tensor("rtmp_s", [128, 512], BF16))
    knat = e(nc.sbuf_tensor("knat_s", [128, 512], BF16))
    qrope = e(nc.sbuf_tensor("qrope_s", [128, 512], BF16))
    apre = e(nc.sbuf_tensor("apre_s", [128, 1024], BF16))
    wrS = e(nc.sbuf_tensor("wrS_s", [128, 256], BF16))
    wshgS = e(nc.sbuf_tensor("wshgS_s", [128, 16], BF16))
    tin = {}
    for name in ["t_s", "t_q1", "t_q2", "t_q3", "t_q4", "t_m", "t_m2", "t_v",
                 "t_sd", "t_r", "r_mx", "r_sum", "r_rin"]:
        tin[name] = e(nc.sbuf_tensor(name, [128, 1], F32))
    r_p = e(nc.sbuf_tensor("r_p", [128, NE], F32))
    r_pn = e(nc.sbuf_tensor("r_pn", [128, NE], F32))
    r_gt = e(nc.sbuf_tensor("r_gt", [128, NE], F32))
    r_cnt = e(nc.sbuf_tensor("r_cnt", [128, NE], F32))
    r_msk = e(nc.sbuf_tensor("r_msk", [128, NE], F32))

    for k, val in enumerate([1.0 / 240.0, 30.0, -30.0, 1e-5, -1.0]):
        nc.const_aps.register(F32, val, cst[:, k:k + 1])

    POOL_B = 137216
    pool = e(nc.sbuf_tensor("pool", [128, POOL_B // 4], F32))

    def pf(off_b, nelem):
        assert off_b % 4 == 0 and off_b + nelem * 4 <= POOL_B
        return pool.ap()[:, off_b // 4: off_b // 4 + nelem]

    def pb(off_b, nelem):
        assert off_b % 4 == 0 and off_b + nelem * 2 <= POOL_B
        v = pool.ap()[:, off_b // 4: off_b // 4 + (nelem + 1) // 2]
        return v.bitcast(BF16)[:, 0:nelem]

    PSA = e(nc.psum_tensor("PSA", [128, 4096], F32))

    def bank(b, lo=0, n=512):
        return PSA[:, b * 512 + lo: b * 512 + lo + n]

    def bankP(b, p0, p1, lo=0, n=512):
        return PSA[p0:p1, b * 512 + lo: b * 512 + lo + n]

    def bankB(b):
        return PSA.ap()[:, b * 512:(b + 1) * 512].bitcast(BF16)

    P = Prog(nc, stack)
    T, V, A, SY, G = nc.tensor, nc.vector, nc.scalar, nc.sync, nc.gpsimd

    def emit_ln(x_ap, out_ap, dump=(5, 6)):
        b0, b1 = dump
        A.activation(bank(b0), x_ap[:, 0:512], AF.Square, accum_out=tin["t_q1"][:])
        A.activation(bank(b1), x_ap[:, 512:1024], AF.Square, accum_out=tin["t_q2"][:])
        A.activation(bank(b0), x_ap[:, 1024:1536], AF.Square, accum_out=tin["t_q3"][:])
        i4 = A.activation(bank(b1), x_ap[:, 1536:2048], AF.Square, accum_out=tin["t_q4"][:])
        c_act = P.inc("sACT", i4)
        P.vt(V.reduce_sum(tin["t_s"][:], x_ap, axis=AX.X))
        P.vt(V.tensor_scalar_mul(tin["t_m"][:], tin["t_s"][:], 1.0 / D))
        P.vt(V.tensor_mul(tin["t_m2"][:], tin["t_m"][:], tin["t_m"][:]))
        P.wait("sACT", V, c_act)
        P.vt(V.tensor_add(tin["t_q1"][:], tin["t_q1"][:], tin["t_q2"][:]))
        P.vt(V.tensor_add(tin["t_q3"][:], tin["t_q3"][:], tin["t_q4"][:]))
        P.vt(V.tensor_add(tin["t_q1"][:], tin["t_q1"][:], tin["t_q3"][:]))
        i5 = V.tensor_scalar(tin["t_v"][:], tin["t_q1"][:], 1.0 / D, tin["t_m2"][:],
                             ALU.mult, ALU.subtract)
        c_dve = P.inc("sDVE", i5)
        P.wait("sDVE", A, c_dve)
        c_a2 = P.inc("sACT", A.activation(tin["t_sd"][:], tin["t_v"][:], AF.Sqrt, bias=1e-5))
        P.wait("sACT", V, c_a2)
        P.vt(V.reciprocal(tin["t_r"][:], tin["t_sd"][:]))
        return V.tensor_scalar(out_ap, x_ap, tin["t_m"][:], tin["t_r"][:],
                               ALU.subtract, ALU.mult)

    def emit_rope(psum_in, tmp_bf, ctab, stab, out_bf):
        src = psum_in.rearrange("p (c two) -> p c two", two=2)
        dst = tmp_bf.rearrange("p (c two) -> p c two", two=2)
        V.tensor_scalar_mul(dst[:, :, 0:1], src[:, :, 1:2], -1.0)
        P.vt(V.tensor_copy(dst[:, :, 1:2], src[:, :, 0:1]))
        cb = ctab.unsqueeze(1).broadcast_to([128, 8, HD])
        sb = stab.unsqueeze(1).broadcast_to([128, 8, HD])
        ps3 = psum_in.rearrange("p (g d) -> p g d", g=8)
        tm3 = tmp_bf.rearrange("p (g d) -> p g d", g=8)
        ou3 = out_bf.rearrange("p (g d) -> p g d", g=8)
        P.vt(V.tensor_tensor(tm3, tm3, sb, ALU.mult))
        P.vt(V.tensor_tensor(ou3, ps3, cb, ALU.mult))
        return V.tensor_tensor(ou3, ou3, tm3, ALU.add)

    def finish_early():
        for name in list(P.sem):
            P.wait(name, SY)
        for j in range(TPC):
            i = SY.dma_start(out_h.ap()[j * 128:(j + 1) * 128, :],
                             resid[:, j * D:(j + 1) * D])
            P.inc("sdX", i, 16)
        SY.wait_ge(P.sem["sdX"][0], P.n("sdX"))
        stack.close()
        return nc

    O_WKS, O_WVS = 0, 16384
    O_XST0, O_XST1 = 32768, 40960
    O_XNT0, O_XNT1 = 49152, 53248
    O_XN = 57344
    O_KNAT, O_RTMP = 61440, 62464
    O_KST0, O_KST1 = 63488, 65536
    O_VST = 67584
    O_QT = 71680
    O_WQB, O_XNT4 = 0, 16384
    O_CTQ, O_STQ = 32768, 34816
    O_QROPE = 36864
    O_WRK = 0
    O_EXPS = 2048
    O_BOS = 6144
    O_AOUT = 14336
    O_TMP1 = 22528
    O_WOK = 30720
    O_WOL = 38912
    O_XN2T = 0
    O_YACC = 32768
    O_AT = 98304
    O_WGU = 120832
    O_WDR = 133120
    O_XN2 = 98304
    O_TMP2 = 120832

    WkS = pb(O_WKS, 8192)
    WvS = pb(O_WVS, 8192)
    xst = [pf(O_XST0, 2048), pf(O_XST1, 2048)]
    xnt = [pb(O_XNT0, 2048), pb(O_XNT1, 2048)]
    xn = pb(O_XN, 2048)
    dbgs = pf(O_KST0, 512)
    kst = [pb(O_KST0, 1024), pb(O_KST1, 1024)]
    vst = pb(O_VST, 520)
    QT = pb(O_QT, 32768)

    P.inc("sdX", SY.dma_start(cst[:], cst_h.ap()), 16)
    P.inc("sdX", SY.dma_start(idn[:], idn_h.ap()), 16)
    P.inc("sdX", SY.dma_start(mask0[:], mask_h.ap()), 16)
    for j in range(TPC):
        P.inc("sdX", SY.dma_start(resid[:, j * D:(j + 1) * D], xo_h.ap()[j * 128:(j + 1) * 128, :]), 16)
    i = SY.dma_start(kscr_h.ap()[:, 64:69, 0:1024], augk_h.ap())
    c_const = P.inc("sdX", i, 16)
    for kc in range(16):
        P.inc("sdW", G.dma_start(WkS[:, kc * 512:(kc + 1) * 512], wk_h.ap()[:, kc * 512:(kc + 1) * 512]), 16)
        i = G.dma_start(WvS[:, kc * 512:(kc + 1) * 512], wv_h.ap()[:, kc * 512:(kc + 1) * 512])
        c_wkv = P.inc("sdW", i, 16)
    V.memset(ones[:], 1.0)
    V.memset(vst[:, :], 1.0)

    if ph == 11:
        return finish_early()

    P.wait("sdW", T, c_wkv)
    P.wait("sdX", T, c_const)

    c_xload = {}
    c_xn_t = {}
    c_vp_t = {}
    c_krope_t = {}
    kv_dma_done = {}
    i = SY.dma_start(xst[0][:, :], x_h.ap()[0:128, :])
    c_xload[0] = P.inc("sdX", i, 16)

    nt_lim = 2 if ph in (12, 13, 14, 15, 16, 17, 18, 19) else NT
    segl = {13: 1, 14: 2, 15: 3, 16: 4, 17: 5, 18: 2, 19: 2}.get(ph, 9)
    for ki in range(nt_lim):
        par = ki % 2
        if ki + 1 < NT:
            if ki >= 1:
                P.wait("sDVE", SY, c_xn_t[ki - 1])
            i = SY.dma_start(xst[(ki + 1) % 2][:, :], x_h.ap()[(ki + 1) * 128:(ki + 2) * 128, :])
            c_xload[ki + 1] = P.inc("sdX", i, 16)
        if ki >= 1 and (ki - 1) in c_krope_t:
            P.wait("sDVE", SY, c_krope_t[ki - 1])
        P.inc("sdX", SY.dma_start(ctabK[:], cosn_h.ap()[ki * 128:(ki + 1) * 128, :]), 16)
        i = SY.dma_start(stabK[:], sinn_h.ap()[ki * 128:(ki + 1) * 128, :])
        c_tab = P.inc("sdX", i, 16)

        P.wait("sdX", V, c_xload[ki])
        P.wait("sdX", A, c_xload[ki])
        if ki >= 1 and (ki - 1) in c_vp_t:
            P.wait("sPE", A, c_vp_t[ki - 1])
        i_norm = emit_ln(xst[par][:, :], xn[:, :])
        c_xn_t[ki] = P.inc("sDVE", i_norm)
        if DEBUG:
            P.wait("sDVE", SY, c_xn_t[ki])
            P.inc("sdX", SY.dma_start(dbg_xn.ap()[ki], xn[:, :]), 16)
        if segl < 2:
            continue
        P.wait("sDVE", T, c_xn_t[ki])
        for kc in range(16):
            i = T.transpose(bankB(kc // 8)[:, (kc % 8) * 128:(kc % 8) * 128 + 128],
                            xn[:, kc * 128:kc * 128 + 128], idn[:])
        c_tp = P.inc("sPE", i)
        if ph == 19:
            continue
        P.wait("sPE", V, c_tp)
        lastV = None
        for kc in range(16):
            src = bankB(kc // 8)[:, (kc % 8) * 128:(kc % 8) * 128 + 128]
            dst = xnt[par][:, kc * 128:kc * 128 + 128]
            lastV = V.tensor_copy(dst, src)
        c_evA = P.n("sACT")
        c_evV = P.inc("sDVE", lastV)
        if segl < 3:
            continue
        P.wait("sACT", T, c_evA)
        P.wait("sDVE", T, c_evV)
        for kc in range(16):
            i = T.matmul(bank(2), xnt[par][:, kc * 128:kc * 128 + 128],
                         WkS[:, kc * 512:(kc + 1) * 512], start=(kc == 0), stop=(kc == 15))
        c_kp = P.inc("sPE", i)
        if segl < 4:
            continue
        P.wait("sPE", V, c_kp)
        if DEBUG:
            c_d = P.inc("sDVE", V.tensor_copy(dbgs[:, :], bank(2)))
            P.wait("sDVE", SY, c_d)
            P.inc("sdX", SY.dma_start(dbg_kp.ap()[ki], dbgs[:, :]), 16)
        P.wait("sdX", V, c_tab)
        i_rope = emit_rope(bank(2), rtmp[:, :], ctabK.ap(), stabK.ap(), knat[:, :])
        c_krope_t[ki] = P.inc("sDVE", i_rope)
        if DEBUG:
            P.wait("sDVE", SY, c_krope_t[ki])
            P.inc("sdX", SY.dma_start(dbg_kn.ap()[ki], knat[:, :]), 16)
        if segl < 5:
            continue
        P.wait("sDVE", T, c_krope_t[ki])
        for cch in range(4):
            i = T.transpose(bankB(3)[:, cch * 128:cch * 128 + 128],
                            knat[:, cch * 128:cch * 128 + 128], idn[:])
        c_kt = P.inc("sPE", i)
        P.wait("sPE", V, c_kt)
        if ki >= 2 and (ki - 2) in kv_dma_done:
            P.wait("sdX", V, kv_dma_done[ki - 2])
        for cch in range(4):
            src = bankB(3)[:, cch * 128:cch * 128 + 128]
            V.tensor_copy(kst[par][0:64, (2 * cch) * 128:(2 * cch) * 128 + 128], src[0:64, :])
            i = V.tensor_copy(kst[par][0:64, (2 * cch + 1) * 128:(2 * cch + 1) * 128 + 128], src[64:128, :])
        c_kev = P.inc("sDVE", i)
        if segl < 6:
            continue
        for kc in range(16):
            i = T.matmul(bank(2), xnt[par][:, kc * 128:kc * 128 + 128],
                         WvS[:, kc * 512:(kc + 1) * 512], start=(kc == 0), stop=(kc == 15))
        c_vp_t[ki] = P.inc("sPE", i)
        P.wait("sPE", V, c_vp_t[ki])
        if ki >= 1 and (ki - 1) in kv_dma_done:
            P.wait("sdX", V, kv_dma_done[ki - 1])
        i = V.tensor_copy(vst.rearrange("p (g c) -> p g c", g=8)[:, :, 0:64],
                          bank(2).rearrange("p (g c) -> p g c", g=8))
        c_vev = P.inc("sDVE", i)
        P.wait("sDVE", SY, c_kev)
        P.inc("sdX", SY.dma_start(kscr_h.ap()[ki, 0:64, 0:1024], kst[par][0:64, :]), 16)
        P.wait("sDVE", SY, c_vev)
        i = SY.dma_start(kscr_h.ap()[ki, :, 1024:1544], vst[:, :])
        kv_dma_done[ki] = P.inc("sdX", i, 16)

    if ph in (1, 12, 13, 14, 15, 16, 17, 18, 19):
        return finish_early()
    wqb = pb(O_WQB, 8192)
    xnt4 = pb(O_XNT4, 8192)
    ctq = pf(O_CTQ, 512)
    stq = pf(O_STQ, 512)

    for j in range(TPC):
        P.inc("sdX", SY.dma_start(ctq[:, j * 64:(j + 1) * 64], coso_h.ap()[j * 128:(j + 1) * 128, :]), 16)
        P.inc("sdX", SY.dma_start(stq[:, j * 64:(j + 1) * 64], sino_h.ap()[j * 128:(j + 1) * 128, :]), 16)
    i = SY.dma_start(QT[64:69, :], augq_h.ap())
    c_qc = P.inc("sdX", i, 16)

    c_qr_par = {}
    c_qtev = {}
    for h in range(2):
        c_ln4 = 0
        for j4 in range(4):
            j = h * 4 + j4
            i_norm = emit_ln(resid.ap()[:, j * D:(j + 1) * D], xn[:, :])
            c_xnq = P.inc("sDVE", i_norm)
            P.wait("sDVE", T, c_xnq)
            for kc in range(16):
                i = T.transpose(bankB(kc // 8)[:, (kc % 8) * 128:(kc % 8) * 128 + 128],
                                xn[:, kc * 128:kc * 128 + 128], idn[:])
            c_tp = P.inc("sPE", i)
            P.wait("sPE", V, c_tp)
            for kc in range(16):
                src = bankB(kc // 8)[:, (kc % 8) * 128:(kc % 8) * 128 + 128]
                i = V.tensor_copy(xnt4[:, kc * 512 + j4 * 128: kc * 512 + j4 * 128 + 128], src)
            c_ln4 = P.inc("sDVE", i)
        for n in range(4):
            if h == 0 and n == 0:
                P.wait("sPE", G, c_vp_t[NT - 1])
            else:
                P.wait("sPE", G)
            for kc in range(16):
                i = G.dma_start(wqb[:, kc * 512:(kc + 1) * 512], wq_h.ap()[n][:, kc * 512:(kc + 1) * 512])
                c_wq = P.inc("sdW", i, 16)
            P.wait("sdW", T, c_wq)
            P.wait("sDVE", T, c_ln4)
            for j4 in range(4):
                j = h * 4 + j4
                bq = 2 + (j4 % 2)
                if (j4 % 2) in c_qr_par:
                    P.wait("sDVE", T, c_qr_par[j4 % 2])
                for kc in range(16):
                    i = T.matmul(bank(bq), xnt4[:, kc * 512 + j4 * 128: kc * 512 + j4 * 128 + 128],
                                 wqb[:, kc * 512:(kc + 1) * 512], start=(kc == 0), stop=(kc == 15))
                c_qp = P.inc("sPE", i)
                P.wait("sPE", V, c_qp)
                P.wait("sdX", V, c_qc)
                i_rope = emit_rope(bank(bq), rtmp[:, :],
                                   ctq.rearrange("p (j d) -> p j d", j=TPC)[:, j, :],
                                   stq.rearrange("p (j d) -> p j d", j=TPC)[:, j, :],
                                   qrope[:, :])
                c_qr_par[j4 % 2] = P.inc("sDVE", i_rope)
                P.wait("sDVE", T, c_qr_par[j4 % 2])
                if j4 in c_qtev:
                    P.wait("sDVE", T, c_qtev[j4])
                for cch in range(4):
                    i = T.transpose(bankB(7)[:, cch * 128:cch * 128 + 128],
                                    qrope[:, cch * 128:cch * 128 + 128], idn[:])
                c_qt = P.inc("sPE", i)
                P.wait("sPE", V, c_qt)
                for cch in range(4):
                    src = bankB(7)[:, cch * 128:cch * 128 + 128]
                    hh = 8 * n + 2 * cch
                    V.tensor_copy(QT[0:64, hh * 1024 + j * 128: hh * 1024 + j * 128 + 128], src[0:64, :])
                    i = V.tensor_copy(QT[0:64, (hh + 1) * 1024 + j * 128: (hh + 1) * 1024 + j * 128 + 128], src[64:128, :])
                c_qtev[j4] = P.inc("sDVE", i)

    c_q_done_pe = P.n("sPE")
    c_q_done_dve = P.n("sDVE")
    if DEBUG:
        P.wait("sDVE", SY, c_q_done_dve)
        P.wait("sdX", SY)
        P.inc("sdX", SY.dma_start(dbg_qt.ap(), QT[0:69, :]), 16)

    if ph == 2:
        return finish_early()
    wrk = [pb(O_WRK, 386), pb(O_WRK + 1024, 386)]
    expS = [pb(O_EXPS, 1024), pb(O_EXPS + 2048, 1024)]
    boS = pf(O_BOS, 2048)
    aout = pf(O_AOUT, 2048)
    tmp1 = pf(O_TMP1, 2048)
    woK = [pb(O_WOK + k * 4096, 2048) for k in range(2)]
    WoL = pb(O_WOL, 16384)

    P.wait("sPE", SY, c_q_done_pe)
    i = SY.dma_start(boS, bo_h.ap())
    c_bo = P.inc("sdX", i, 16)

    qt4 = QT[0:69, :].rearrange("p (h t s) -> p h t s", h=NH, t=TPC)
    mb = mask0.ap().unsqueeze(1).broadcast_to([128, 8, 128])

    engs = [nc.tensor, nc.scalar, nc.vector, nc.sync]
    idx = [eng.partition_id() for eng in engs]

    snap0 = P.snapshot()
    end_snap = None
    for ci in nc.Switch(engines=engs, index=idx, n=NCORE):
        P.restore(snap0)
        tiles = core_tiles(ci)
        c_prev_wave_ctx = 0
        for j, qi in enumerate(tiles):
            kis = [qi] + list(range(qi))
            nk = len(kis)
            for w in range(4):
                c_exp = {}
                c_ctx = {}
                c_start = P.n("sDVE")
                c_rb = {}
                def emit_ctx(k2):
                    p2 = k2 % 2
                    P.wait("sACT", T, c_exp[k2])
                    if k2 == 0:
                        P.wait("sDVE", T, c_msk)
                    last = None
                    for gg in range(2):
                        for hh in range(4):
                            last = T.matmul(bankP(gg, 0, 65, hh * 128, 128),
                                            wrk[p2][:, 256 + gg * 65:256 + gg * 65 + 65],
                                            expS[p2][:, (gg * 4 + hh) * 128:(gg * 4 + hh) * 128 + 128],
                                            start=(k2 == 0 and hh == 0),
                                            stop=(k2 == nk - 1),
                                            skip_group_check=True)
                    c_ctx[k2] = P.inc("sPE", last)

                for kidx, ki in enumerate(kis):
                    par = kidx % 2
                    if kidx >= 2:
                        P.wait("sPE", SY, c_ctx[kidx - 2])
                    else:
                        P.wait("sPE", SY, c_prev_wave_ctx)
                    P.inc("sdX", SY.dma_start(wrk[par][0:69, 0:256], kscr_h.ap()[ki, 0:69, w * 256:(w + 1) * 256]), 16)
                    i = SY.dma_start(wrk[par][:, 256:386],
                                     kscr_h.ap()[ki, :, 1024 + w * 130:1024 + (w + 1) * 130])
                    c_dma = P.inc("sdX", i, 16)
                    sb = 2 + 2 * par
                    P.wait("sdX", T, c_dma)
                    if kidx >= 2:
                        P.wait("sACT", T, c_exp[kidx - 2])
                    if kidx < 2:
                        P.wait("sDVE", T, c_start)
                    for gg in range(2):
                        i = T.matmul(bank(sb + gg), wrk[par][0:69, gg * 128:gg * 128 + 128],
                                     qt4[:, 4 * (2 * w + gg):4 * (2 * w + gg) + 4, j, :],
                                     start=True, stop=True)
                    c_sc = P.inc("sPE", i)
                    P.wait("sPE", A, c_sc)
                    if kidx >= 2:
                        P.wait("sPE", A, c_ctx[kidx - 2])
                    A.activation(PSA[:, sb * 512:(sb + 2) * 512], PSA[:, sb * 512:(sb + 2) * 512],
                                 AF.Tanh, scale=1.0 / 240.0)
                    i = A.activation(expS[par][:, :], PSA[:, sb * 512:(sb + 2) * 512],
                                     AF.Exp, bias=-30.0, scale=30.0)
                    c_exp[kidx] = P.inc("sACT", i)
                    if kidx == 0:
                        P.wait("sACT", V, c_exp[0])
                        i = V.tensor_tensor(expS[par][:, :].rearrange("p (g s) -> p g s", g=8),
                                            expS[par][:, :].rearrange("p (g s) -> p g s", g=8),
                                            mb, ALU.mult)
                        c_msk = P.inc("sDVE", i)
                    if kidx >= 1:
                        emit_ctx(kidx - 1)
                emit_ctx(nk - 1)
                c_prev_wave_ctx = c_ctx[nk - 1]
                if DEBUG and j == 1 and w == 0:
                    P.wait("sPE", V, c_ctx[nk - 1])
                    c_d2 = P.vt(V.tensor_copy(dbgs[:, :], PSA[:, 0:512]))
                    P.wait("sVT", SY, c_d2)
                    P.inc("sdX", SY.dma_start(dbg_c2.ap()[:, 0:512], dbgs[0:65, :]), 16)
                    c_d2 = P.vt(V.tensor_copy(dbgs[:, :], PSA[:, 512:1024]))
                    P.wait("sVT", SY, c_d2)
                    P.inc("sdX", SY.dma_start(dbg_c2.ap()[:, 512:1024], dbgs[0:65, :]), 16)
                    P.wait("sdX", V)
                if DEBUG and j == 0:
                    P.wait("sPE", V, c_ctx[nk - 1])
                    c_dd = P.vt(V.tensor_copy(dbgs[:, :], PSA[:, 0:512]))
                    P.wait("sVT", SY, c_dd)
                    P.inc("sdX", SY.dma_start(dbg_cx.ap()[w, :, 0:512], dbgs[0:65, :]), 16)
                    c_dd = P.vt(V.tensor_copy(dbgs[:, :], PSA[:, 512:1024]))
                    P.wait("sVT", SY, c_dd)
                    i_dd = SY.dma_start(dbg_cx.ap()[w, :, 512:1024], dbgs[0:65, :])
                    P.inc("sdX", i_dd, 16)
                    P.wait("sdX", V)
                P.wait("sPE", V, c_ctx[nk - 1])
                for gg in range(2):
                    for hh in range(4):
                        hloc = 4 * gg + hh
                        h = 8 * w + hloc
                        c_r = P.vt(V.reciprocal(rsb[:, :], bankP(gg, 64, 65, hh * 128, 128)))
                        T.wait_ge(P.sem["sVT"][0], c_r)
                        if hloc >= 4:
                            P.wait("sDVE", T, c_rb[hloc - 4])
                        i = T.matmul(bankP(6, 0, 64, (hloc % 4) * 128, 128),
                                     ones.ap()[0:1, 0:64], rsb.ap()[0:1, :],
                                     start=True, stop=True)
                        c_b = P.inc("sPE", i)
                        P.wait("sPE", V, c_b)
                        c_rb[hloc] = P.vt(V.tensor_copy(rbsb[:, :], bankP(6, 0, 64, (hloc % 4) * 128, 128)))
                        i = V.tensor_tensor(WoL[(h % 2) * 64:(h % 2) * 64 + 64,
                                                j * 2048 + (h // 2) * 128: j * 2048 + (h // 2) * 128 + 128],
                                            bankP(gg, 0, 64, hh * 128, 128), rbsb[:, :], ALU.mult)
                        c_rb[hloc] = P.inc("sDVE", i)
        if end_snap is None:
            end_snap = P.snapshot()
        else:
            assert DEBUG or end_snap == P.snapshot(), "switch body semaphore mismatch"
    P.restore(end_snap)
    c_wol_done = P.n("sDVE")
    if DEBUG:
        P.wait("sDVE", SY, c_wol_done)
        P.inc("sdX", SY.dma_start(dbg_wol.ap(), WoL[:, :]), 16)

    if ph == 3:
        return finish_early()
    c_wo_ev = 0
    c_wok_cons = {}
    nwok = 0
    for jj in range(4):
        for kc in range(16):
            if nwok == 0:
                P.wait("sDVE", G, c_q_done_dve)
            if nwok >= 2:
                P.wait("sPE", G, c_wok_cons[nwok - 2])
            i = G.dma_start(woK[nwok % 2][:, :], wo_h.ap()[:, kc * 2048:(kc + 1) * 2048])
            c_wk2 = P.inc("sdW", i, 16)
            P.wait("sdW", T, c_wk2)
            if kc == 0:
                P.wait("sDVE", T, c_wol_done if jj == 0 else c_wo_ev)
            for j2 in range(2):
                for n in range(4):
                    i = T.matmul(bank(4 * j2 + n), WoL[:, (2 * jj + j2) * 2048 + kc * 128:(2 * jj + j2) * 2048 + kc * 128 + 128],
                                 woK[nwok % 2][:, n * 512:(n + 1) * 512],
                                 start=(kc == 0), stop=(kc == 15), skip_group_check=True)
            c_wok_cons[nwok] = P.inc("sPE", i)
            nwok += 1
        c_wom = c_wok_cons[nwok - 1]
        for j2 in range(2):
            j = 2 * jj + j2
            P.wait("sPE", V, c_wom)
            P.wait("sdX", V, c_bo)
            for n in range(4):
                i = V.tensor_tensor(aout[:, n * 512:(n + 1) * 512], bank(4 * j2 + n),
                                    boS[:, n * 512:(n + 1) * 512], ALU.add)
            c_ev4 = P.inc("sDVE", i)
            P.wait("sDVE", A, c_ev4)
            if DEBUG:
                P.wait("sDVE", SY, c_ev4)
                P.inc("sdX", SY.dma_start(dbg_ao.ap()[j], aout[:, :]), 16)
                P.wait("sdX", V)
            i_norm = emit_ln(aout[:, :], tmp1[:, :], dump=(4 * j2, 4 * j2 + 1))
            P.vt(i_norm)
            i = V.tensor_tensor(resid.ap()[:, j * D:(j + 1) * D],
                                resid.ap()[:, j * D:(j + 1) * D], tmp1[:, :], ALU.add)
            c_wo_ev = P.inc("sDVE", i)

    c_p1_dve = P.n("sDVE")
    c_p1_pe = P.n("sPE")
    if DEBUG:
        P.wait("sDVE", SY, c_p1_dve)
        P.inc("sdX", SY.dma_start(dbg_ra.ap(), resid[:, :]), 16)

    if ph == 4:
        return finish_early()
    xn2T = pb(O_XN2T, 16384)
    yacc = pf(O_YACC, 16384)
    aT = pb(O_AT, 11264)
    wgu = [pb(O_WGU + k * 4096, 2048) for k in range(3)]
    wdr = [pb(O_WDR + k * 1024, 512) for k in range(4)]
    xn2 = pb(O_XN2, 2048)
    tmp2 = pf(O_TMP2, 2048)

    P.wait("sPE", G, c_p1_pe)
    P.inc("sdW", G.dma_start(wrS[:, :], wr_h.ap()), 16)
    i = G.dma_start(wshgS[:, :], wshg_h.ap())
    c_wr = P.inc("sdW", i, 16)

    c_xn2t = 0
    c_comb_prev = 0
    c_sg_prev = 0
    for j in range(TPC):
        P.wait("sDVE", V, c_p1_dve)
        P.wait("sDVE", A, c_p1_dve)
        i_norm = emit_ln(resid.ap()[:, j * D:(j + 1) * D], xn2[:, :], dump=(0, 1))
        c_xnj = P.inc("sDVE", i_norm)
        if DEBUG:
            P.wait("sDVE", SY, c_xnj)
            P.inc("sdX", SY.dma_start(dbg_x2.ap()[j], xn2[:, :]), 16)
            P.wait("sdX", T)
        P.wait("sDVE", T, c_xnj)
        for kc in range(16):
            i = T.transpose(bankB(4 + kc // 8)[:, (kc % 8) * 128:(kc % 8) * 128 + 128],
                            xn2[:, kc * 128:kc * 128 + 128], idn[:])
        c_tp = P.inc("sPE", i)
        P.wait("sPE", V, c_tp)
        for kc in range(16):
            src = bankB(4 + kc // 8)[:, (kc % 8) * 128:(kc % 8) * 128 + 128]
            i = V.tensor_copy(xn2T[:, kc * 1024 + j * 128: kc * 1024 + j * 128 + 128], src)
        c_xn2t = P.inc("sDVE", i)
        P.wait("sdW", T, c_wr)
        P.wait("sDVE", T, c_xn2t)
        if j >= 1:
            P.wait("sDVE", T, c_comb_prev)
            P.wait("sACT", T, c_sg_prev)
        for kc in range(16):
            T.matmul(bank(2, 0, NE), xn2T[:, kc * 1024 + j * 128: kc * 1024 + j * 128 + 128],
                     wrS[:, kc * 16:(kc + 1) * 16], start=(kc == 0), stop=(kc == 15))
        for kc in range(16):
            i = T.matmul(bank(3, 0, 1), xn2T[:, kc * 1024 + j * 128: kc * 1024 + j * 128 + 128],
                         wshgS[:, kc:kc + 1], start=(kc == 0), stop=(kc == 15))
        c_rt = P.inc("sPE", i)
        P.wait("sPE", V, c_rt)
        P.wait("sPE", A, c_rt)
        if DEBUG:
            c_dl = P.vt(V.tensor_copy(dbgs[:, 0:16], bank(2, 0, NE)))
            c_dl = P.vt(V.tensor_copy(dbgs[:, 16:17], bank(3, 0, 1)))
            P.wait("sVT", SY, c_dl)
            P.inc("sdX", SY.dma_start(dbg_lg.ap()[j], dbgs[:, 0:17]), 16)
            P.wait("sdX", V)
        c_mx = P.vt(V.reduce_max(tin["r_mx"][:], bank(2, 0, NE), axis=AX.X, negate=True))
        A.wait_ge(P.sem["sVT"][0], c_mx)
        if j >= 1:
            P.wait("sDVE", A, c_comb_prev)
        i = A.activation(r_p[:], bank(2, 0, NE), AF.Exp, bias=tin["r_mx"][:])
        c_e = P.inc("sACT", i)
        i = A.activation(sgt[:, j:j + 1], bank(3, 0, 1), AF.Sigmoid)
        c_sg_prev = P.inc("sACT", i)
        P.wait("sACT", V, c_e)
        P.vt(V.reduce_sum(tin["r_sum"][:], r_p[:], axis=AX.X))
        P.vt(V.reciprocal(tin["r_rin"][:], tin["r_sum"][:]))
        P.vt(V.tensor_scalar_mul(r_pn[:], r_p[:], tin["r_rin"][:]))
        for ex in range(NE):
            P.vt(V.tensor_single_scalar(r_gt[:], r_p[:], r_p[:, ex:ex + 1], ALU.is_gt))
            P.vt(V.reduce_sum(r_cnt[:, ex:ex + 1], r_gt[:], axis=AX.X))
        P.vt(V.tensor_single_scalar(r_msk[:], r_cnt[:], 3.5, ALU.is_lt))
        i = V.tensor_tensor(comb[:, j * NE:(j + 1) * NE], r_pn[:], r_msk[:], ALU.mult)
        c_comb_prev = P.inc("sDVE", i)
        i = V.memset(yacc[:, j * D:(j + 1) * D], 0.0)
        P.inc("sDVE", i)
    c_passA = P.n("sDVE")
    c_passA_act = P.n("sACT")

    if ph == 5:
        return finish_early()
    gu_loads = []
    gu_cons = []
    wd_cons = []
    n_load = 0
    n_wd = 0
    c_aT = {}
    c_dn_ev = {}
    c_prev_done = c_passA
    for e_i in range(NE + 1):
        shared = (e_i == NE)
        for st in range(4 if shared else 1):
            nm = 11
            for m in range(nm):
                for which in range(2):
                    if n_load >= 3:
                        P.wait("sPE", G, gu_cons[n_load - 3])
                    if shared:
                        src = (wsg_h if which == 0 else wsu_h).ap()[st * 11 + m]
                    else:
                        src = (wg_h if which == 0 else wu_h).ap()[e_i, m]
                    i = G.dma_start(wgu[n_load % 3][:, :], src)
                    gu_loads.append(P.inc("sdW", i, 16))
                    n_load += 1
                bset = 0 if m % 2 == 0 else 4
                P.wait("sdW", T, gu_loads[n_load - 1])
                if m >= 2:
                    P.wait("sDVE", T, c_aT[m - 2])
                else:
                    P.wait("sDVE", T, c_prev_done)
                    if (nm - 2 + m) in c_aT:
                        P.wait("sDVE", T, c_aT[nm - 2 + m])
                for which in range(2):
                    wb = wgu[(n_load - 2 + which) % 3]
                    for kc in range(16):
                        for hh in range(2):
                            i = T.matmul(bank(bset + 2 * which + hh),
                                         wb[:, kc * 128:kc * 128 + 128],
                                         xn2T[:, kc * 1024 + hh * 512: kc * 1024 + hh * 512 + 512],
                                         start=(kc == 0), stop=(kc == 15))
                c_gu = P.inc("sPE", i)
                gu_cons.append(c_gu)
                gu_cons.append(c_gu)
                P.wait("sPE", A, c_gu)
                i = A.activation(apre[:, :], PSA[:, bset * 512:(bset + 2) * 512], AF.Silu)
                c_si = P.inc("sACT", i)
                P.wait("sACT", V, c_si)
                i = V.tensor_tensor(aT[:, m * 1024:(m + 1) * 1024], apre[:, :],
                                    PSA[:, (bset + 2) * 512:(bset + 4) * 512], ALU.mult)
                c_aT[m] = P.inc("sDVE", i)
            if DEBUG and e_i == 0 and st == 0:
                P.wait("sDVE", SY, c_aT[nm - 1])
                P.inc("sdX", SY.dma_start(dbg_at.ap(), aT[:, :]), 16)
                P.wait("sdX", T)
            P.wait("sDVE", T, c_aT[nm - 1])
            for n in range(4):
                for kc in range(11):
                    if n_wd >= 4:
                        P.wait("sPE", G, wd_cons[n_wd - 4])
                    if shared:
                        src = wsd_h.ap()[n, st * 11 + kc]
                    else:
                        src = wd_h.ap()[e_i, n, kc]
                    i = G.dma_start(wdr[n_wd % 4][:, :], src)
                    c_wdl = P.inc("sdW", i, 16)
                    P.wait("sdW", T, c_wdl)
                    last = None
                    for tt in range(TPC):
                        if kc == 0:
                            prev = c_dn_ev.get((n - 1, tt)) if n >= 1 else c_dn_ev.get((3, tt))
                            if prev is not None:
                                P.wait("sDVE", T, prev)
                        last = T.matmul(bank(tt), aT[:, kc * 1024 + tt * 128: kc * 1024 + tt * 128 + 128],
                                        wdr[n_wd % 4][:, :], start=(kc == 0), stop=(kc == 10),
                                        skip_group_check=True)
                        if kc == 10:
                            cfin = P.inc("sPE", last)
                            P.wait("sPE", V, cfin)
                            scal = sgt[:, tt:tt + 1] if shared else comb[:, tt * NE + e_i:tt * NE + e_i + 1]
                            i = V.scalar_tensor_tensor(yacc[:, tt * D + n * 512: tt * D + n * 512 + 512],
                                                       bank(tt), scal,
                                                       yacc[:, tt * D + n * 512: tt * D + n * 512 + 512],
                                                       ALU.mult, ALU.add)
                            c_dn_ev[(n, tt)] = P.inc("sDVE", i)
                    if kc != 10:
                        wd_cons.append(P.inc("sPE", last))
                    else:
                        wd_cons.append(P.n("sPE"))
                    n_wd += 1
            c_prev_done = c_dn_ev[(3, TPC - 1)]
            if DEBUG:
                di = e_i * 1 + st if shared else e_i
                P.wait("sDVE", SY, c_prev_done)
                P.inc("sdX", SY.dma_start(dbg_yp.ap()[di], yacc[:, :]), 16)
                P.wait("sdX", V)
                P.wait("sdX", T)

    c_ffn_pe = P.n("sPE")
    c_ffn_dve = P.n("sDVE")
    if DEBUG:
        P.wait("sDVE", SY, c_ffn_dve)
        P.inc("sdX", SY.dma_start(dbg_ya.ap(), yacc[:, :]), 16)

    if ph == 6:
        return finish_early()
    for j in range(TPC):
        P.wait("sDVE", V, c_ffn_dve)
        P.wait("sDVE", A, c_ffn_dve)
        i_norm = emit_ln(yacc[:, j * D:(j + 1) * D], tmp2[:, :], dump=(0, 1))
        P.vt(i_norm)
        i = V.tensor_tensor(resid.ap()[:, j * D:(j + 1) * D],
                            resid.ap()[:, j * D:(j + 1) * D], tmp2[:, :], ALU.add)
        P.inc("sDVE", i)
    P.wait("sDVE", SY)
    for j in range(TPC):
        i = SY.dma_start(out_h.ap()[j * 128:(j + 1) * 128, :], resid[:, j * D:(j + 1) * D])
        P.inc("sdX", i, 16)
    SY.wait_ge(P.sem["sdX"][0], P.n("sdX"))

    stack.close()
    return nc



def _sb_chunked(w, kc_rows=128):
    R, C = w.shape
    k = R // kc_rows
    return np.ascontiguousarray(w.reshape(k, kc_rows, C).transpose(1, 0, 2).reshape(kc_rows, k * C))


def _prepare(L0_attn_hidden_states, L0_attn_Wq, L0_attn_Wk, L0_attn_Wv, L0_attn_Wo,
             L0_attn_bo, L0_ffn_w_router, L0_ffn_w_gate, L0_ffn_w_up, L0_ffn_w_down,
             L0_ffn_ws_gate, L0_ffn_ws_up, L0_ffn_ws_down, L0_ffn_w_shared_gate,
             ph=7):
    x = np.ascontiguousarray(np.asarray(L0_attn_hidden_states, np.float32).reshape(S, D))
    wq = np.asarray(L0_attn_Wq, np.float32)
    wk = np.asarray(L0_attn_Wk, np.float32)
    wv = np.asarray(L0_attn_Wv, np.float32)
    wo = np.asarray(L0_attn_Wo, np.float32)
    bo = np.asarray(L0_attn_bo, np.float32)
    wr = np.asarray(L0_ffn_w_router, np.float32)
    wg = np.asarray(L0_ffn_w_gate, np.float32)
    wu = np.asarray(L0_ffn_w_up, np.float32)
    wd = np.asarray(L0_ffn_w_down, np.float32)
    wsg = np.asarray(L0_ffn_ws_gate, np.float32)
    wsu = np.asarray(L0_ffn_ws_up, np.float32)
    wsd = np.asarray(L0_ffn_ws_down, np.float32)
    wshg = np.asarray(L0_ffn_w_shared_gate, np.float32)

    con = build_constants()

    wq_s = np.stack([_sb_chunked(wq[:, n * 512:(n + 1) * 512]) for n in range(4)])
    wk_s = _sb_chunked(wk)
    wv_s = _sb_chunked(wv)
    wo_s = _sb_chunked(wo)
    wr_s = _sb_chunked(wr)
    wshg_s = _sb_chunked(wshg)
    wg_s = np.stack([[_sb_chunked(wg[e][:, m * 128:(m + 1) * 128]) for m in range(11)]
                     for e in range(NE)])
    wu_s = np.stack([[_sb_chunked(wu[e][:, m * 128:(m + 1) * 128]) for m in range(11)]
                     for e in range(NE)])
    wd_s = np.ascontiguousarray(
        wd.reshape(NE, 11, 128, 4, 512).transpose(0, 3, 1, 2, 4))
    wsg_s = np.stack([_sb_chunked(wsg[:, mg * 128:(mg + 1) * 128]) for mg in range(44)])
    wsu_s = np.stack([_sb_chunked(wsu[:, mg * 128:(mg + 1) * 128]) for mg in range(44)])
    wsd_s = np.ascontiguousarray(
        wsd.reshape(44, 128, 4, 512).transpose(2, 0, 1, 3))
    bo_b = np.ascontiguousarray(np.broadcast_to(bo[None, :], (128, D)))

    nc = build_nc(ph)

    shared = dict(x=x, wq_s=wq_s, wk_s=wk_s, wv_s=wv_s, wo_s=wo_s, bo_b=bo_b,
                  wr_s=wr_s, wg_s=wg_s, wu_s=wu_s, wd_s=wd_s, wsg_s=wsg_s,
                  wsu_s=wsu_s, wsd_s=wsd_s, wshg_s=wshg_s,
                  cosn=con["cosn"], sinn=con["sinn"], augk=con["augk"],
                  mask0=con["mask0"], idn=con["idn"],
                  cst=np.ascontiguousarray(np.broadcast_to(
                      np.array([1.0 / 240.0, 30.0, -30.0, 1e-5, -1.0, 0, 0, 0],
                               np.float32)[None, :], (128, 8))))

    in_maps = []
    for c in range(NCORE):
        tiles = core_tiles(c)
        rows = np.concatenate([np.arange(t * 128, (t + 1) * 128) for t in tiles])
        m = dict(shared)
        m["x_own"] = np.ascontiguousarray(x[rows])
        m["cos_own"] = np.ascontiguousarray(con["cosn"][rows])
        m["sin_own"] = np.ascontiguousarray(con["sinn"][rows])
        m["augq"] = build_augq(c, con)
        in_maps.append(m)
    return nc, in_maps


def kernel(**inputs):
    nc, in_maps = _prepare(**inputs)
    res = run_bass_kernel_spmd(nc, in_maps, core_ids=list(range(NCORE)))

    out = np.empty((S, D), np.float32)
    for c in range(NCORE):
        shard = np.asarray(res.results[c]["out_shard"], np.float32)
        for j, t in enumerate(core_tiles(c)):
            out[t * 128:(t + 1) * 128] = shard[j * 128:(j + 1) * 128]
    return out.reshape(1, S, D)


# version 05 (final)
# speedup vs baseline: 1.4401x; 1.4401x over previous
from contextlib import ExitStack

import numpy as np
import ml_dtypes

import concourse.bass as bass
import concourse.mybir as mybir
from concourse.bass_utils import run_bass_kernel_spmd

F32 = mybir.dt.float32
BF16 = mybir.dt.bfloat16
AF = mybir.ActivationFunctionType
ALU = mybir.AluOpType
AX = mybir.AxisListType
BF = ml_dtypes.bfloat16

S, D, NH, NG, HD = 8192, 2048, 32, 8, 64
NT, TPC = 64, 8
NE, I_EXP, I_SH = 16, 1408, 5632
NCORE = 8
DEBUG = 0


def core_tiles(c):
    return [c, 15 - c, 16 + c, 31 - c, 32 + c, 47 - c, 48 + c, 63 - c]


def _bf(x):
    return np.asarray(x, dtype=BF)



def build_constants():
    con = {}
    inv = 1.0 / 10000.0 ** (np.arange(0, HD, 2, dtype=np.float64) / HD)
    fr = np.arange(S, dtype=np.float64)[:, None] * inv[None, :]
    con["cosn"] = np.repeat(np.cos(fr), 2, axis=1).astype(np.float32)
    con["sinn"] = np.repeat(np.sin(fr), 2, axis=1).astype(np.float32)
    slopes = 2.0 ** (-(np.arange(1, NH + 1, dtype=np.float64)) * 8.0 / NH)
    s_hi = _bf(slopes).astype(np.float64)
    s_lo = _bf(slopes - s_hi).astype(np.float64)
    con["slope_hi"], con["slope_lo"], con["slope_eff"] = s_hi, s_lo, s_hi + s_lo
    t = np.arange(S, dtype=np.float64).reshape(NT, 128)
    t_hi = _bf(t).astype(np.float64)
    t_lo = t - t_hi
    augk = np.zeros((NT, 5, 8 * 128), np.float64)
    for g in range(8):
        sl = slice(g * 128, g * 128 + 128)
        augk[:, 0, sl] = t_hi
        augk[:, 1, sl] = t_lo
        augk[:, 2, sl] = t_hi
        augk[:, 3, sl] = 1.0
        augk[:, 4, sl] = 1.0
    con["augk"] = _bf(augk)
    kk = np.arange(128)
    con["mask0"] = _bf((kk[None, :] >= kk[:, None]).astype(np.float32))
    con["idn"] = np.eye(128, dtype=BF)
    return con


def build_augq(c, con):
    tiles = core_tiles(c)
    aq = np.zeros((5, NH * 1024), np.float64)
    for h in range(NH):
        for j, qt in enumerate(tiles):
            sl = slice(h * 1024 + j * 128, h * 1024 + j * 128 + 128)
            s_glob = qt * 128 + np.arange(128, dtype=np.float64)
            X = -con["slope_eff"][h] * s_glob
            X_hi = _bf(X).astype(np.float64)
            aq[0, sl] = con["slope_hi"][h]
            aq[1, sl] = con["slope_hi"][h]
            aq[2, sl] = con["slope_lo"][h]
            aq[3, sl] = X_hi
            aq[4, sl] = X - X_hi
    return _bf(aq)



class Prog:
    def __init__(self, nc, stack):
        self.nc = nc
        self.sem = {}
        for name in ["sdX", "sdW", "sPE", "sACT", "sDVE", "sVT"]:
            self.sem[name] = [stack.enter_context(nc.semaphore(name)), 0]

    def inc(self, name, inst, k=1):
        s = self.sem[name]
        inst.then_inc(s[0], k)
        s[1] += k
        return s[1]

    def n(self, name):
        return self.sem[name][1]

    def wait(self, name, eng, at=None):
        s = self.sem[name]
        eng.wait_ge(s[0], s[1] if at is None else at)

    def vt(self, inst):
        self.inc("sVT", inst)
        self.nc.vector.wait_ge(self.sem["sVT"][0], self.sem["sVT"][1])
        return self.sem["sVT"][1]

    def snapshot(self):
        return {k: v[1] for k, v in self.sem.items()}

    def restore(self, snap):
        for k, v in snap.items():
            self.sem[k][1] = v


def build_nc(ph=7):
    nc = bass.Bass("TRN2", target_bir_lowering=False, debug=False,
                   detect_race_conditions=False)

    def din(name, shape, dtype=F32):
        return nc.dram_tensor(name, list(shape), dtype, kind="ExternalInput")

    x_h = din("x", (S, D))
    xo_h = din("x_own", (TPC * 128, D))
    wq_h = din("wq_s", (4, 128, 16 * 512))
    wk_h = din("wk_s", (128, 16 * 512))
    wv_h = din("wv_s", (128, 16 * 512))
    wo_h = din("wo_s", (128, 16 * 2048))
    bo_h = din("bo_b", (128, D))
    wr_h = din("wr_s", (128, 16 * 16))
    wg_h = din("wg_s", (NE, 11, 128, 16 * 128))
    wu_h = din("wu_s", (NE, 11, 128, 16 * 128))
    wd_h = din("wd_s", (NE, 4, 11, 128, 512))
    wsg_h = din("wsg_s", (44, 128, 16 * 128))
    wsu_h = din("wsu_s", (44, 128, 16 * 128))
    wsd_h = din("wsd_s", (4, 44, 128, 512))
    wshg_h = din("wshg_s", (128, 16))
    cosn_h = din("cosn", (S, HD))
    sinn_h = din("sinn", (S, HD))
    coso_h = din("cos_own", (TPC * 128, HD))
    sino_h = din("sin_own", (TPC * 128, HD))
    augq_h = din("augq", (5, NH * 1024), BF16)
    augk_h = din("augk", (NT, 5, 1024), BF16)
    mask_h = din("mask0", (128, 128), BF16)
    idn_h = din("idn", (128, 128), BF16)
    cst_h = din("cst", (128, 8))

    out_h = nc.dram_tensor("out_shard", [TPC * 128, D], F32, kind="ExternalOutput")
    kscr_h = nc.dram_tensor("kscr", [NT, 128, 1544], BF16, kind="Internal")
    if DEBUG:
        dbg_xn = nc.dram_tensor("dbg_xn", [NT, 128, D], BF16, kind="Internal")
        dbg_kn = nc.dram_tensor("dbg_kn", [NT, 128, 512], BF16, kind="Internal")
        dbg_kp = nc.dram_tensor("dbg_kp", [NT, 128, 512], F32, kind="Internal")
        dbg_qt = nc.dram_tensor("dbg_qt", [69, NH * 1024], BF16, kind="Internal")
        dbg_ra = nc.dram_tensor("dbg_ra", [128, TPC * D], F32, kind="Internal")
        dbg_wol = nc.dram_tensor("dbg_wol", [128, TPC * 2048], BF16, kind="Internal")
        dbg_ao = nc.dram_tensor("dbg_ao", [TPC, 128, D], F32, kind="Internal")
        dbg_es = nc.dram_tensor("dbg_es", [4, 128, 1024], BF16, kind="Internal")
        dbg_cx = nc.dram_tensor("dbg_cx", [4, 65, 1024], F32, kind="Internal")
        dbg_e2 = nc.dram_tensor("dbg_e2", [16, 128, 1024], BF16, kind="Internal")
        dbg_c2 = nc.dram_tensor("dbg_c2", [65, 1024], F32, kind="Internal")
        dbg_c3 = nc.dram_tensor("dbg_c3", [16, 65, 512], F32, kind="Internal")
        dbg_lg = nc.dram_tensor("dbg_lg", [TPC, 128, 17], F32, kind="Internal")
        dbg_x2 = nc.dram_tensor("dbg_x2", [TPC, 128, D], BF16, kind="Internal")
        dbg_ya = nc.dram_tensor("dbg_ya", [128, TPC * D], F32, kind="Internal")
        dbg_at = nc.dram_tensor("dbg_at", [128, 11 * 1024], BF16, kind="Internal")
        dbg_yp = nc.dram_tensor("dbg_yp", [20, 128, TPC * D], F32, kind="Internal")

    stack = ExitStack()
    e = stack.enter_context

    resid = e(nc.sbuf_tensor("resid", [128, TPC * D], F32))
    cst = e(nc.sbuf_tensor("cst_s", [128, 8], F32))
    idn = e(nc.sbuf_tensor("idn_s", [128, 128], BF16))
    mask0 = e(nc.sbuf_tensor("mask0_s", [128, 128], BF16))
    ones = e(nc.sbuf_tensor("ones_s", [1, 128], F32))
    ctabK = e(nc.sbuf_tensor("ctabK", [128, HD], F32))
    stabK = e(nc.sbuf_tensor("stabK", [128, HD], F32))
    comb = e(nc.sbuf_tensor("comb_s", [128, TPC * NE], F32))
    sgt = e(nc.sbuf_tensor("sg_s", [128, TPC], F32))
    rsb = e(nc.sbuf_tensor("rsb", [1, 128], F32))
    rbsb = e(nc.sbuf_tensor("rbsb", [64, 128], F32))
    rtmp = e(nc.sbuf_tensor("rtmp_s", [128, 512], BF16))
    knat = e(nc.sbuf_tensor("knat_s", [128, 512], BF16))
    qrope = e(nc.sbuf_tensor("qrope_s", [128, 512], BF16))
    apre = e(nc.sbuf_tensor("apre_s", [128, 1024], BF16))
    wrS = e(nc.sbuf_tensor("wrS_s", [128, 256], BF16))
    wshgS = e(nc.sbuf_tensor("wshgS_s", [128, 16], BF16))
    tin = {}
    for name in ["t_s", "t_q1", "t_q2", "t_q3", "t_q4", "t_m", "t_m2", "t_v",
                 "t_sd", "t_r", "r_mx", "r_sum", "r_rin"]:
        tin[name] = e(nc.sbuf_tensor(name, [128, 1], F32))
    r_p = e(nc.sbuf_tensor("r_p", [128, NE], F32))
    r_pn = e(nc.sbuf_tensor("r_pn", [128, NE], F32))
    r_gt = e(nc.sbuf_tensor("r_gt", [128, NE], F32))
    r_cnt = e(nc.sbuf_tensor("r_cnt", [128, NE], F32))
    r_msk = e(nc.sbuf_tensor("r_msk", [128, NE], F32))

    for k, val in enumerate([1.0 / 240.0, 30.0, -30.0, 1e-5, -1.0]):
        nc.const_aps.register(F32, val, cst[:, k:k + 1])

    POOL_B = 137216
    pool = e(nc.sbuf_tensor("pool", [128, POOL_B // 4], F32))

    def pf(off_b, nelem):
        assert off_b % 4 == 0 and off_b + nelem * 4 <= POOL_B
        return pool.ap()[:, off_b // 4: off_b // 4 + nelem]

    def pb(off_b, nelem):
        assert off_b % 4 == 0 and off_b + nelem * 2 <= POOL_B
        v = pool.ap()[:, off_b // 4: off_b // 4 + (nelem + 1) // 2]
        return v.bitcast(BF16)[:, 0:nelem]

    PSA = e(nc.psum_tensor("PSA", [128, 4096], F32))

    def bank(b, lo=0, n=512):
        return PSA[:, b * 512 + lo: b * 512 + lo + n]

    def bankP(b, p0, p1, lo=0, n=512):
        return PSA[p0:p1, b * 512 + lo: b * 512 + lo + n]

    def bankB(b):
        return PSA.ap()[:, b * 512:(b + 1) * 512].bitcast(BF16)

    P = Prog(nc, stack)
    T, V, A, SY, G = nc.tensor, nc.vector, nc.scalar, nc.sync, nc.gpsimd

    def emit_ln(x_ap, out_ap, dump=(5, 6)):
        b0, b1 = dump
        A.activation(bank(b0), x_ap[:, 0:512], AF.Square, accum_out=tin["t_q1"][:])
        A.activation(bank(b1), x_ap[:, 512:1024], AF.Square, accum_out=tin["t_q2"][:])
        A.activation(bank(b0), x_ap[:, 1024:1536], AF.Square, accum_out=tin["t_q3"][:])
        i4 = A.activation(bank(b1), x_ap[:, 1536:2048], AF.Square, accum_out=tin["t_q4"][:])
        c_act = P.inc("sACT", i4)
        P.vt(V.reduce_sum(tin["t_s"][:], x_ap, axis=AX.X))
        P.vt(V.tensor_scalar_mul(tin["t_m"][:], tin["t_s"][:], 1.0 / D))
        P.vt(V.tensor_mul(tin["t_m2"][:], tin["t_m"][:], tin["t_m"][:]))
        P.wait("sACT", V, c_act)
        P.vt(V.tensor_add(tin["t_q1"][:], tin["t_q1"][:], tin["t_q2"][:]))
        P.vt(V.tensor_add(tin["t_q3"][:], tin["t_q3"][:], tin["t_q4"][:]))
        P.vt(V.tensor_add(tin["t_q1"][:], tin["t_q1"][:], tin["t_q3"][:]))
        i5 = V.tensor_scalar(tin["t_v"][:], tin["t_q1"][:], 1.0 / D, tin["t_m2"][:],
                             ALU.mult, ALU.subtract)
        c_dve = P.inc("sDVE", i5)
        P.wait("sDVE", A, c_dve)
        c_a2 = P.inc("sACT", A.activation(tin["t_sd"][:], tin["t_v"][:], AF.Sqrt, bias=1e-5))
        P.wait("sACT", V, c_a2)
        P.vt(V.reciprocal(tin["t_r"][:], tin["t_sd"][:]))
        return V.tensor_scalar(out_ap, x_ap, tin["t_m"][:], tin["t_r"][:],
                               ALU.subtract, ALU.mult)

    def emit_rope(psum_in, tmp_bf, ctab, stab, out_bf):
        src = psum_in.rearrange("p (c two) -> p c two", two=2)
        dst = tmp_bf.rearrange("p (c two) -> p c two", two=2)
        V.tensor_scalar_mul(dst[:, :, 0:1], src[:, :, 1:2], -1.0)
        P.vt(V.tensor_copy(dst[:, :, 1:2], src[:, :, 0:1]))
        cb = ctab.unsqueeze(1).broadcast_to([128, 8, HD])
        sb = stab.unsqueeze(1).broadcast_to([128, 8, HD])
        ps3 = psum_in.rearrange("p (g d) -> p g d", g=8)
        tm3 = tmp_bf.rearrange("p (g d) -> p g d", g=8)
        ou3 = out_bf.rearrange("p (g d) -> p g d", g=8)
        P.vt(V.tensor_tensor(tm3, tm3, sb, ALU.mult))
        P.vt(V.tensor_tensor(ou3, ps3, cb, ALU.mult))
        return V.tensor_tensor(ou3, ou3, tm3, ALU.add)

    def finish_early():
        for name in list(P.sem):
            P.wait(name, SY)
        for j in range(TPC):
            i = SY.dma_start(out_h.ap()[j * 128:(j + 1) * 128, :],
                             resid[:, j * D:(j + 1) * D])
            P.inc("sdX", i, 16)
        SY.wait_ge(P.sem["sdX"][0], P.n("sdX"))
        stack.close()
        return nc

    O_WKS, O_WVS = 0, 16384
    O_XST0, O_XST1 = 32768, 40960
    O_XNT0, O_XNT1 = 49152, 53248
    O_XN = 57344
    O_KNAT, O_RTMP = 61440, 62464
    O_KST0, O_KST1 = 63488, 65536
    O_VST = 67584
    O_QT = 71680
    O_WQB, O_XNT4 = 0, 16384
    O_CTQ, O_STQ = 32768, 34816
    O_QROPE = 36864
    O_WRK = 0
    O_EXPS = 2048
    O_BOS = 6144
    O_AOUT = 14336
    O_TMP1 = 22528
    O_WOK = 30720
    O_WOL = 38912
    O_XN2T = 0
    O_YACC = 32768
    O_AT = 98304
    O_WGU = 120832
    O_WDR = 133120
    O_XN2 = 98304
    O_TMP2 = 120832

    WkS = pb(O_WKS, 8192)
    WvS = pb(O_WVS, 8192)
    xst = [pf(O_XST0, 2048), pf(O_XST1, 2048)]
    xnt = [pb(O_XNT0, 2048), pb(O_XNT1, 2048)]
    xn = pb(O_XN, 2048)
    dbgs = pf(O_KST0, 512)
    kst = [pb(O_KST0, 1024), pb(O_KST1, 1024)]
    vst = pb(O_VST, 520)
    QT = pb(O_QT, 32768)

    P.inc("sdX", SY.dma_start(cst[:], cst_h.ap()), 16)
    P.inc("sdX", SY.dma_start(idn[:], idn_h.ap()), 16)
    P.inc("sdX", SY.dma_start(mask0[:], mask_h.ap()), 16)
    for j in range(TPC):
        P.inc("sdX", SY.dma_start(resid[:, j * D:(j + 1) * D], xo_h.ap()[j * 128:(j + 1) * 128, :]), 16)
    i = SY.dma_start(kscr_h.ap()[:, 64:69, 0:1024], augk_h.ap())
    c_const = P.inc("sdX", i, 16)
    for kc in range(16):
        P.inc("sdW", G.dma_start(WkS[:, kc * 512:(kc + 1) * 512], wk_h.ap()[:, kc * 512:(kc + 1) * 512]), 16)
        i = G.dma_start(WvS[:, kc * 512:(kc + 1) * 512], wv_h.ap()[:, kc * 512:(kc + 1) * 512])
        c_wkv = P.inc("sdW", i, 16)
    V.memset(ones[:], 1.0)
    V.memset(vst[:, :], 1.0)

    if ph == 11:
        return finish_early()

    P.wait("sdW", T, c_wkv)
    P.wait("sdX", T, c_const)

    c_xload = {}
    c_xn_t = {}
    c_vp_t = {}
    c_krope_t = {}
    kv_dma_done = {}
    i = SY.dma_start(xst[0][:, :], x_h.ap()[0:128, :])
    c_xload[0] = P.inc("sdX", i, 16)

    nt_lim = 2 if ph in (12, 13, 14, 15, 16, 17, 18, 19) else NT
    segl = {13: 1, 14: 2, 15: 3, 16: 4, 17: 5, 18: 2, 19: 2}.get(ph, 9)
    for ki in range(nt_lim):
        par = ki % 2
        if ki + 1 < NT:
            if ki >= 1:
                P.wait("sDVE", SY, c_xn_t[ki - 1])
            i = SY.dma_start(xst[(ki + 1) % 2][:, :], x_h.ap()[(ki + 1) * 128:(ki + 2) * 128, :])
            c_xload[ki + 1] = P.inc("sdX", i, 16)
        if ki >= 1 and (ki - 1) in c_krope_t:
            P.wait("sDVE", SY, c_krope_t[ki - 1])
        P.inc("sdX", SY.dma_start(ctabK[:], cosn_h.ap()[ki * 128:(ki + 1) * 128, :]), 16)
        i = SY.dma_start(stabK[:], sinn_h.ap()[ki * 128:(ki + 1) * 128, :])
        c_tab = P.inc("sdX", i, 16)

        P.wait("sdX", V, c_xload[ki])
        P.wait("sdX", A, c_xload[ki])
        if ki >= 1 and (ki - 1) in c_vp_t:
            P.wait("sPE", A, c_vp_t[ki - 1])
        i_norm = emit_ln(xst[par][:, :], xn[:, :])
        c_xn_t[ki] = P.inc("sDVE", i_norm)
        if DEBUG:
            P.wait("sDVE", SY, c_xn_t[ki])
            P.inc("sdX", SY.dma_start(dbg_xn.ap()[ki], xn[:, :]), 16)
        if segl < 2:
            continue
        P.wait("sDVE", T, c_xn_t[ki])
        for kc in range(16):
            i = T.transpose(bankB(kc // 8)[:, (kc % 8) * 128:(kc % 8) * 128 + 128],
                            xn[:, kc * 128:kc * 128 + 128], idn[:])
        c_tp = P.inc("sPE", i)
        if ph == 19:
            continue
        P.wait("sPE", V, c_tp)
        lastV = None
        for kc in range(16):
            src = bankB(kc // 8)[:, (kc % 8) * 128:(kc % 8) * 128 + 128]
            dst = xnt[par][:, kc * 128:kc * 128 + 128]
            lastV = V.tensor_copy(dst, src)
        c_evA = P.n("sACT")
        c_evV = P.inc("sDVE", lastV)
        if segl < 3:
            continue
        P.wait("sACT", T, c_evA)
        P.wait("sDVE", T, c_evV)
        for kc in range(16):
            i = T.matmul(bank(2), xnt[par][:, kc * 128:kc * 128 + 128],
                         WkS[:, kc * 512:(kc + 1) * 512], start=(kc == 0), stop=(kc == 15))
        c_kp = P.inc("sPE", i)
        if segl < 4:
            continue
        P.wait("sPE", V, c_kp)
        if DEBUG:
            c_d = P.inc("sDVE", V.tensor_copy(dbgs[:, :], bank(2)))
            P.wait("sDVE", SY, c_d)
            P.inc("sdX", SY.dma_start(dbg_kp.ap()[ki], dbgs[:, :]), 16)
        P.wait("sdX", V, c_tab)
        i_rope = emit_rope(bank(2), rtmp[:, :], ctabK.ap(), stabK.ap(), knat[:, :])
        c_krope_t[ki] = P.inc("sDVE", i_rope)
        if DEBUG:
            P.wait("sDVE", SY, c_krope_t[ki])
            P.inc("sdX", SY.dma_start(dbg_kn.ap()[ki], knat[:, :]), 16)
        if segl < 5:
            continue
        P.wait("sDVE", T, c_krope_t[ki])
        for cch in range(4):
            i = T.transpose(bankB(3)[:, cch * 128:cch * 128 + 128],
                            knat[:, cch * 128:cch * 128 + 128], idn[:])
        c_kt = P.inc("sPE", i)
        P.wait("sPE", V, c_kt)
        if ki >= 2 and (ki - 2) in kv_dma_done:
            P.wait("sdX", V, kv_dma_done[ki - 2])
        for cch in range(4):
            src = bankB(3)[:, cch * 128:cch * 128 + 128]
            V.tensor_copy(kst[par][0:64, (2 * cch) * 128:(2 * cch) * 128 + 128], src[0:64, :])
            i = V.tensor_copy(kst[par][0:64, (2 * cch + 1) * 128:(2 * cch + 1) * 128 + 128], src[64:128, :])
        c_kev = P.inc("sDVE", i)
        if segl < 6:
            continue
        for kc in range(16):
            i = T.matmul(bank(2), xnt[par][:, kc * 128:kc * 128 + 128],
                         WvS[:, kc * 512:(kc + 1) * 512], start=(kc == 0), stop=(kc == 15))
        c_vp_t[ki] = P.inc("sPE", i)
        P.wait("sPE", V, c_vp_t[ki])
        if ki >= 1 and (ki - 1) in kv_dma_done:
            P.wait("sdX", V, kv_dma_done[ki - 1])
        i = V.tensor_copy(vst.rearrange("p (g c) -> p g c", g=8)[:, :, 0:64],
                          bank(2).rearrange("p (g c) -> p g c", g=8))
        c_vev = P.inc("sDVE", i)
        P.wait("sDVE", SY, c_kev)
        P.inc("sdX", SY.dma_start(kscr_h.ap()[ki, 0:64, 0:1024], kst[par][0:64, :]), 16)
        P.wait("sDVE", SY, c_vev)
        i = SY.dma_start(kscr_h.ap()[ki, :, 1024:1544], vst[:, :])
        kv_dma_done[ki] = P.inc("sdX", i, 16)

    if ph in (1, 12, 13, 14, 15, 16, 17, 18, 19):
        return finish_early()
    wqb = pb(O_WQB, 8192)
    xnt4 = pb(O_XNT4, 8192)
    ctq = pf(O_CTQ, 512)
    stq = pf(O_STQ, 512)

    for j in range(TPC):
        P.inc("sdX", SY.dma_start(ctq[:, j * 64:(j + 1) * 64], coso_h.ap()[j * 128:(j + 1) * 128, :]), 16)
        P.inc("sdX", SY.dma_start(stq[:, j * 64:(j + 1) * 64], sino_h.ap()[j * 128:(j + 1) * 128, :]), 16)
    i = SY.dma_start(QT[64:69, :], augq_h.ap())
    c_qc = P.inc("sdX", i, 16)

    c_qr_par = {}
    c_qtev = {}
    for h in range(2):
        c_ln4 = 0
        for j4 in range(4):
            j = h * 4 + j4
            i_norm = emit_ln(resid.ap()[:, j * D:(j + 1) * D], xn[:, :])
            c_xnq = P.inc("sDVE", i_norm)
            P.wait("sDVE", T, c_xnq)
            for kc in range(16):
                i = T.transpose(bankB(kc // 8)[:, (kc % 8) * 128:(kc % 8) * 128 + 128],
                                xn[:, kc * 128:kc * 128 + 128], idn[:])
            c_tp = P.inc("sPE", i)
            P.wait("sPE", V, c_tp)
            for kc in range(16):
                src = bankB(kc // 8)[:, (kc % 8) * 128:(kc % 8) * 128 + 128]
                i = V.tensor_copy(xnt4[:, kc * 512 + j4 * 128: kc * 512 + j4 * 128 + 128], src)
            c_ln4 = P.inc("sDVE", i)
        for n in range(4):
            if h == 0 and n == 0:
                P.wait("sPE", G, c_vp_t[NT - 1])
            else:
                P.wait("sPE", G)
            for kc in range(16):
                i = G.dma_start(wqb[:, kc * 512:(kc + 1) * 512], wq_h.ap()[n][:, kc * 512:(kc + 1) * 512])
                c_wq = P.inc("sdW", i, 16)
            P.wait("sdW", T, c_wq)
            P.wait("sDVE", T, c_ln4)
            for j4 in range(4):
                j = h * 4 + j4
                bq = 2 + (j4 % 2)
                if (j4 % 2) in c_qr_par:
                    P.wait("sDVE", T, c_qr_par[j4 % 2])
                for kc in range(16):
                    i = T.matmul(bank(bq), xnt4[:, kc * 512 + j4 * 128: kc * 512 + j4 * 128 + 128],
                                 wqb[:, kc * 512:(kc + 1) * 512], start=(kc == 0), stop=(kc == 15))
                c_qp = P.inc("sPE", i)
                P.wait("sPE", V, c_qp)
                P.wait("sdX", V, c_qc)
                i_rope = emit_rope(bank(bq), rtmp[:, :],
                                   ctq.rearrange("p (j d) -> p j d", j=TPC)[:, j, :],
                                   stq.rearrange("p (j d) -> p j d", j=TPC)[:, j, :],
                                   qrope[:, :])
                c_qr_par[j4 % 2] = P.inc("sDVE", i_rope)
                P.wait("sDVE", T, c_qr_par[j4 % 2])
                if j4 in c_qtev:
                    P.wait("sDVE", T, c_qtev[j4])
                for cch in range(4):
                    i = T.transpose(bankB(7)[:, cch * 128:cch * 128 + 128],
                                    qrope[:, cch * 128:cch * 128 + 128], idn[:])
                c_qt = P.inc("sPE", i)
                P.wait("sPE", V, c_qt)
                for cch in range(4):
                    src = bankB(7)[:, cch * 128:cch * 128 + 128]
                    hh = 8 * n + 2 * cch
                    V.tensor_copy(QT[0:64, hh * 1024 + j * 128: hh * 1024 + j * 128 + 128], src[0:64, :])
                    i = V.tensor_copy(QT[0:64, (hh + 1) * 1024 + j * 128: (hh + 1) * 1024 + j * 128 + 128], src[64:128, :])
                c_qtev[j4] = P.inc("sDVE", i)

    c_q_done_pe = P.n("sPE")
    c_q_done_dve = P.n("sDVE")
    if DEBUG:
        P.wait("sDVE", SY, c_q_done_dve)
        P.wait("sdX", SY)
        P.inc("sdX", SY.dma_start(dbg_qt.ap(), QT[0:69, :]), 16)

    if ph == 2:
        return finish_early()
    wrk = [pb(O_WRK, 386), pb(O_WRK + 1024, 386),
           pb(O_AOUT, 386), pb(O_AOUT + 1024, 386)]
    expS = [pb(O_EXPS, 1024), pb(O_EXPS + 2048, 1024)]
    boS = pf(O_BOS, 2048)
    aout = pf(O_AOUT, 2048)
    tmp1 = pf(O_TMP1, 2048)
    woK = [pb(O_WOK + k * 4096, 2048) for k in range(2)]
    WoL = pb(O_WOL, 16384)

    P.wait("sPE", SY, c_q_done_pe)
    i = SY.dma_start(boS, bo_h.ap())
    c_bo = P.inc("sdX", i, 16)

    qt4 = QT[0:69, :].rearrange("p (h t s) -> p h t s", h=NH, t=TPC)
    mb = mask0.ap().unsqueeze(1).broadcast_to([128, 8, 128])

    engs = [nc.tensor, nc.scalar, nc.vector, nc.sync]
    idx = [eng.partition_id() for eng in engs]

    snap0 = P.snapshot()
    end_snap = None
    for ci in nc.Switch(engines=engs, index=idx, n=NCORE):
        P.restore(snap0)
        tiles = core_tiles(ci)
        c_prev_wave_ctx = 0
        for j, qi in enumerate(tiles):
            kis = [qi] + list(range(qi))
            nk = len(kis)
            for w in range(4):
                c_exp = {}
                c_ctx = {}
                c_start = P.n("sDVE")
                c_rb = {}
                def emit_ctx(k2):
                    p2 = k2 % 2
                    P.wait("sACT", T, c_exp[k2])
                    if k2 == 0:
                        P.wait("sDVE", T, c_msk)
                    last = None
                    for gg in range(2):
                        for hh in range(4):
                            last = T.matmul(bankP(gg, 0, 65, hh * 128, 128),
                                            wrk[k2 % 4][:, 256 + gg * 65:256 + gg * 65 + 65],
                                            expS[p2][:, (gg * 4 + hh) * 128:(gg * 4 + hh) * 128 + 128],
                                            start=(k2 == 0 and hh == 0),
                                            stop=(k2 == nk - 1),
                                            skip_group_check=True)
                    c_ctx[k2] = P.inc("sPE", last)

                for kidx, ki in enumerate(kis):
                    par = kidx % 2
                    w4 = kidx % 4
                    if kidx >= 4:
                        P.wait("sPE", SY, c_ctx[kidx - 4])
                    else:
                        P.wait("sPE", SY, c_prev_wave_ctx)
                    P.inc("sdX", SY.dma_start(wrk[w4][0:69, 0:256], kscr_h.ap()[ki, 0:69, w * 256:(w + 1) * 256]), 16)
                    i = SY.dma_start(wrk[w4][:, 256:386],
                                     kscr_h.ap()[ki, :, 1024 + w * 130:1024 + (w + 1) * 130])
                    c_dma = P.inc("sdX", i, 16)
                    sb = 2 + 2 * par
                    P.wait("sdX", T, c_dma)
                    if kidx >= 2:
                        P.wait("sACT", T, c_exp[kidx - 2])
                    if kidx < 2:
                        P.wait("sDVE", T, c_start)
                    for gg in range(2):
                        i = T.matmul(bank(sb + gg), wrk[w4][0:69, gg * 128:gg * 128 + 128],
                                     qt4[:, 4 * (2 * w + gg):4 * (2 * w + gg) + 4, j, :],
                                     start=True, stop=True)
                    c_sc = P.inc("sPE", i)
                    P.wait("sPE", A, c_sc)
                    if kidx >= 2:
                        P.wait("sPE", A, c_ctx[kidx - 2])
                    A.activation(PSA[:, sb * 512:(sb + 2) * 512], PSA[:, sb * 512:(sb + 2) * 512],
                                 AF.Tanh, scale=1.0 / 240.0)
                    i = A.activation(expS[par][:, :], PSA[:, sb * 512:(sb + 2) * 512],
                                     AF.Exp, bias=-30.0, scale=30.0)
                    c_exp[kidx] = P.inc("sACT", i)
                    if kidx == 0:
                        P.wait("sACT", V, c_exp[0])
                        i = V.tensor_tensor(expS[par][:, :].rearrange("p (g s) -> p g s", g=8),
                                            expS[par][:, :].rearrange("p (g s) -> p g s", g=8),
                                            mb, ALU.mult)
                        c_msk = P.inc("sDVE", i)
                    if kidx >= 1:
                        emit_ctx(kidx - 1)
                emit_ctx(nk - 1)
                c_prev_wave_ctx = c_ctx[nk - 1]
                if DEBUG and j == 1 and w == 0:
                    P.wait("sPE", V, c_ctx[nk - 1])
                    c_d2 = P.vt(V.tensor_copy(dbgs[:, :], PSA[:, 0:512]))
                    P.wait("sVT", SY, c_d2)
                    P.inc("sdX", SY.dma_start(dbg_c2.ap()[:, 0:512], dbgs[0:65, :]), 16)
                    c_d2 = P.vt(V.tensor_copy(dbgs[:, :], PSA[:, 512:1024]))
                    P.wait("sVT", SY, c_d2)
                    P.inc("sdX", SY.dma_start(dbg_c2.ap()[:, 512:1024], dbgs[0:65, :]), 16)
                    P.wait("sdX", V)
                if DEBUG and j == 0:
                    P.wait("sPE", V, c_ctx[nk - 1])
                    c_dd = P.vt(V.tensor_copy(dbgs[:, :], PSA[:, 0:512]))
                    P.wait("sVT", SY, c_dd)
                    P.inc("sdX", SY.dma_start(dbg_cx.ap()[w, :, 0:512], dbgs[0:65, :]), 16)
                    c_dd = P.vt(V.tensor_copy(dbgs[:, :], PSA[:, 512:1024]))
                    P.wait("sVT", SY, c_dd)
                    i_dd = SY.dma_start(dbg_cx.ap()[w, :, 512:1024], dbgs[0:65, :])
                    P.inc("sdX", i_dd, 16)
                    P.wait("sdX", V)
                P.wait("sPE", V, c_ctx[nk - 1])
                for gg in range(2):
                    for hh in range(4):
                        hloc = 4 * gg + hh
                        h = 8 * w + hloc
                        c_r = P.vt(V.reciprocal(rsb[:, :], bankP(gg, 64, 65, hh * 128, 128)))
                        T.wait_ge(P.sem["sVT"][0], c_r)
                        if hloc >= 4:
                            P.wait("sDVE", T, c_rb[hloc - 4])
                        i = T.matmul(bankP(6, 0, 64, (hloc % 4) * 128, 128),
                                     ones.ap()[0:1, 0:64], rsb.ap()[0:1, :],
                                     start=True, stop=True)
                        c_b = P.inc("sPE", i)
                        P.wait("sPE", V, c_b)
                        c_rb[hloc] = P.vt(V.tensor_copy(rbsb[:, :], bankP(6, 0, 64, (hloc % 4) * 128, 128)))
                        i = V.tensor_tensor(WoL[(h % 2) * 64:(h % 2) * 64 + 64,
                                                j * 2048 + (h // 2) * 128: j * 2048 + (h // 2) * 128 + 128],
                                            bankP(gg, 0, 64, hh * 128, 128), rbsb[:, :], ALU.mult)
                        c_rb[hloc] = P.inc("sDVE", i)
        if end_snap is None:
            end_snap = P.snapshot()
        else:
            assert DEBUG or end_snap == P.snapshot(), "switch body semaphore mismatch"
    P.restore(end_snap)
    c_wol_done = P.n("sDVE")
    if DEBUG:
        P.wait("sDVE", SY, c_wol_done)
        P.inc("sdX", SY.dma_start(dbg_wol.ap(), WoL[:, :]), 16)

    if ph == 3:
        return finish_early()
    c_wo_ev = 0
    c_wok_cons = {}
    nwok = 0
    for jj in range(4):
        for kc in range(16):
            if nwok == 0:
                P.wait("sDVE", G, c_q_done_dve)
            if nwok >= 2:
                P.wait("sPE", G, c_wok_cons[nwok - 2])
            i = G.dma_start(woK[nwok % 2][:, :], wo_h.ap()[:, kc * 2048:(kc + 1) * 2048])
            c_wk2 = P.inc("sdW", i, 16)
            P.wait("sdW", T, c_wk2)
            if kc == 0:
                P.wait("sDVE", T, c_wol_done if jj == 0 else c_wo_ev)
            for j2 in range(2):
                for n in range(4):
                    i = T.matmul(bank(4 * j2 + n), WoL[:, (2 * jj + j2) * 2048 + kc * 128:(2 * jj + j2) * 2048 + kc * 128 + 128],
                                 woK[nwok % 2][:, n * 512:(n + 1) * 512],
                                 start=(kc == 0), stop=(kc == 15), skip_group_check=True)
            c_wok_cons[nwok] = P.inc("sPE", i)
            nwok += 1
        c_wom = c_wok_cons[nwok - 1]
        for j2 in range(2):
            j = 2 * jj + j2
            P.wait("sPE", V, c_wom)
            P.wait("sdX", V, c_bo)
            for n in range(4):
                i = V.tensor_tensor(aout[:, n * 512:(n + 1) * 512], bank(4 * j2 + n),
                                    boS[:, n * 512:(n + 1) * 512], ALU.add)
            c_ev4 = P.inc("sDVE", i)
            P.wait("sDVE", A, c_ev4)
            if DEBUG:
                P.wait("sDVE", SY, c_ev4)
                P.inc("sdX", SY.dma_start(dbg_ao.ap()[j], aout[:, :]), 16)
                P.wait("sdX", V)
            i_norm = emit_ln(aout[:, :], tmp1[:, :], dump=(4 * j2, 4 * j2 + 1))
            P.vt(i_norm)
            i = V.tensor_tensor(resid.ap()[:, j * D:(j + 1) * D],
                                resid.ap()[:, j * D:(j + 1) * D], tmp1[:, :], ALU.add)
            c_wo_ev = P.inc("sDVE", i)

    c_p1_dve = P.n("sDVE")
    c_p1_pe = P.n("sPE")
    if DEBUG:
        P.wait("sDVE", SY, c_p1_dve)
        P.inc("sdX", SY.dma_start(dbg_ra.ap(), resid[:, :]), 16)

    if ph == 4:
        return finish_early()
    xn2T = pb(O_XN2T, 16384)
    yacc = pf(O_YACC, 16384)
    aT = pb(O_AT, 11264)
    wgu = [pb(O_WGU + k * 4096, 2048) for k in range(3)]
    wdr = [pb(O_WDR + k * 1024, 512) for k in range(4)]
    xn2 = pb(O_XN2, 2048)
    tmp2 = pf(O_TMP2, 2048)

    P.wait("sPE", G, c_p1_pe)
    P.inc("sdW", G.dma_start(wrS[:, :], wr_h.ap()), 16)
    i = G.dma_start(wshgS[:, :], wshg_h.ap())
    c_wr = P.inc("sdW", i, 16)

    c_xn2t = 0
    c_comb_prev = 0
    c_sg_prev = 0
    for j in range(TPC):
        P.wait("sDVE", V, c_p1_dve)
        P.wait("sDVE", A, c_p1_dve)
        i_norm = emit_ln(resid.ap()[:, j * D:(j + 1) * D], xn2[:, :], dump=(0, 1))
        c_xnj = P.inc("sDVE", i_norm)
        if DEBUG:
            P.wait("sDVE", SY, c_xnj)
            P.inc("sdX", SY.dma_start(dbg_x2.ap()[j], xn2[:, :]), 16)
            P.wait("sdX", T)
        P.wait("sDVE", T, c_xnj)
        for kc in range(16):
            i = T.transpose(bankB(4 + kc // 8)[:, (kc % 8) * 128:(kc % 8) * 128 + 128],
                            xn2[:, kc * 128:kc * 128 + 128], idn[:])
        c_tp = P.inc("sPE", i)
        P.wait("sPE", V, c_tp)
        for kc in range(16):
            src = bankB(4 + kc // 8)[:, (kc % 8) * 128:(kc % 8) * 128 + 128]
            i = V.tensor_copy(xn2T[:, kc * 1024 + j * 128: kc * 1024 + j * 128 + 128], src)
        c_xn2t = P.inc("sDVE", i)
        P.wait("sdW", T, c_wr)
        P.wait("sDVE", T, c_xn2t)
        if j >= 1:
            P.wait("sDVE", T, c_comb_prev)
            P.wait("sACT", T, c_sg_prev)
        for kc in range(16):
            T.matmul(bank(2, 0, NE), xn2T[:, kc * 1024 + j * 128: kc * 1024 + j * 128 + 128],
                     wrS[:, kc * 16:(kc + 1) * 16], start=(kc == 0), stop=(kc == 15))
        for kc in range(16):
            i = T.matmul(bank(3, 0, 1), xn2T[:, kc * 1024 + j * 128: kc * 1024 + j * 128 + 128],
                         wshgS[:, kc:kc + 1], start=(kc == 0), stop=(kc == 15))
        c_rt = P.inc("sPE", i)
        P.wait("sPE", V, c_rt)
        P.wait("sPE", A, c_rt)
        if DEBUG:
            c_dl = P.vt(V.tensor_copy(dbgs[:, 0:16], bank(2, 0, NE)))
            c_dl = P.vt(V.tensor_copy(dbgs[:, 16:17], bank(3, 0, 1)))
            P.wait("sVT", SY, c_dl)
            P.inc("sdX", SY.dma_start(dbg_lg.ap()[j], dbgs[:, 0:17]), 16)
            P.wait("sdX", V)
        c_mx = P.vt(V.reduce_max(tin["r_mx"][:], bank(2, 0, NE), axis=AX.X, negate=True))
        A.wait_ge(P.sem["sVT"][0], c_mx)
        if j >= 1:
            P.wait("sDVE", A, c_comb_prev)
        i = A.activation(r_p[:], bank(2, 0, NE), AF.Exp, bias=tin["r_mx"][:])
        c_e = P.inc("sACT", i)
        i = A.activation(sgt[:, j:j + 1], bank(3, 0, 1), AF.Sigmoid)
        c_sg_prev = P.inc("sACT", i)
        P.wait("sACT", V, c_e)
        P.vt(V.reduce_sum(tin["r_sum"][:], r_p[:], axis=AX.X))
        P.vt(V.reciprocal(tin["r_rin"][:], tin["r_sum"][:]))
        P.vt(V.tensor_scalar_mul(r_pn[:], r_p[:], tin["r_rin"][:]))
        for ex in range(NE):
            P.vt(V.tensor_single_scalar(r_gt[:], r_p[:], r_p[:, ex:ex + 1], ALU.is_gt))
            P.vt(V.reduce_sum(r_cnt[:, ex:ex + 1], r_gt[:], axis=AX.X))
        P.vt(V.tensor_single_scalar(r_msk[:], r_cnt[:], 3.5, ALU.is_lt))
        i = V.tensor_tensor(comb[:, j * NE:(j + 1) * NE], r_pn[:], r_msk[:], ALU.mult)
        c_comb_prev = P.inc("sDVE", i)
        i = V.memset(yacc[:, j * D:(j + 1) * D], 0.0)
        P.inc("sDVE", i)
    c_passA = P.n("sDVE")
    c_passA_act = P.n("sACT")

    if ph == 5:
        return finish_early()
    gu_loads = []
    gu_cons = []
    wd_cons = []
    n_load = 0
    n_wd = 0
    c_aT = {}
    c_dn_ev = {}
    c_prev_done = c_passA
    for e_i in range(NE + 1):
        shared = (e_i == NE)
        for st in range(4 if shared else 1):
            nm = 11
            for m in range(nm):
                for which in range(2):
                    if n_load >= 3:
                        P.wait("sPE", G, gu_cons[n_load - 3])
                    if shared:
                        src = (wsg_h if which == 0 else wsu_h).ap()[st * 11 + m]
                    else:
                        src = (wg_h if which == 0 else wu_h).ap()[e_i, m]
                    i = G.dma_start(wgu[n_load % 3][:, :], src)
                    gu_loads.append(P.inc("sdW", i, 16))
                    n_load += 1
                bset = 0 if m % 2 == 0 else 4
                P.wait("sdW", T, gu_loads[n_load - 1])
                if m >= 2:
                    P.wait("sDVE", T, c_aT[m - 2])
                else:
                    P.wait("sDVE", T, c_prev_done)
                    if (nm - 2 + m) in c_aT:
                        P.wait("sDVE", T, c_aT[nm - 2 + m])
                for which in range(2):
                    wb = wgu[(n_load - 2 + which) % 3]
                    for kc in range(16):
                        for hh in range(2):
                            i = T.matmul(bank(bset + 2 * which + hh),
                                         wb[:, kc * 128:kc * 128 + 128],
                                         xn2T[:, kc * 1024 + hh * 512: kc * 1024 + hh * 512 + 512],
                                         start=(kc == 0), stop=(kc == 15))
                c_gu = P.inc("sPE", i)
                gu_cons.append(c_gu)
                gu_cons.append(c_gu)
                P.wait("sPE", A, c_gu)
                i = A.activation(apre[:, :], PSA[:, bset * 512:(bset + 2) * 512], AF.Silu)
                c_si = P.inc("sACT", i)
                P.wait("sACT", V, c_si)
                i = V.tensor_tensor(aT[:, m * 1024:(m + 1) * 1024], apre[:, :],
                                    PSA[:, (bset + 2) * 512:(bset + 4) * 512], ALU.mult)
                c_aT[m] = P.inc("sDVE", i)
            if DEBUG and e_i == 0 and st == 0:
                P.wait("sDVE", SY, c_aT[nm - 1])
                P.inc("sdX", SY.dma_start(dbg_at.ap(), aT[:, :]), 16)
                P.wait("sdX", T)
            P.wait("sDVE", T, c_aT[nm - 1])
            for n in range(4):
                for kc in range(11):
                    if n_wd >= 4:
                        P.wait("sPE", G, wd_cons[n_wd - 4])
                    if shared:
                        src = wsd_h.ap()[n, st * 11 + kc]
                    else:
                        src = wd_h.ap()[e_i, n, kc]
                    i = G.dma_start(wdr[n_wd % 4][:, :], src)
                    c_wdl = P.inc("sdW", i, 16)
                    P.wait("sdW", T, c_wdl)
                    last = None
                    for tt in range(TPC):
                        if kc == 0:
                            prev = c_dn_ev.get((n - 1, tt)) if n >= 1 else c_dn_ev.get((3, tt))
                            if prev is not None:
                                P.wait("sDVE", T, prev)
                        last = T.matmul(bank(tt), aT[:, kc * 1024 + tt * 128: kc * 1024 + tt * 128 + 128],
                                        wdr[n_wd % 4][:, :], start=(kc == 0), stop=(kc == 10),
                                        skip_group_check=True)
                        if kc == 10:
                            cfin = P.inc("sPE", last)
                            P.wait("sPE", V, cfin)
                            scal = sgt[:, tt:tt + 1] if shared else comb[:, tt * NE + e_i:tt * NE + e_i + 1]
                            i = V.scalar_tensor_tensor(yacc[:, tt * D + n * 512: tt * D + n * 512 + 512],
                                                       bank(tt), scal,
                                                       yacc[:, tt * D + n * 512: tt * D + n * 512 + 512],
                                                       ALU.mult, ALU.add)
                            c_dn_ev[(n, tt)] = P.inc("sDVE", i)
                    if kc != 10:
                        wd_cons.append(P.inc("sPE", last))
                    else:
                        wd_cons.append(P.n("sPE"))
                    n_wd += 1
            c_prev_done = c_dn_ev[(3, TPC - 1)]
            if DEBUG:
                di = e_i * 1 + st if shared else e_i
                P.wait("sDVE", SY, c_prev_done)
                P.inc("sdX", SY.dma_start(dbg_yp.ap()[di], yacc[:, :]), 16)
                P.wait("sdX", V)
                P.wait("sdX", T)

    c_ffn_pe = P.n("sPE")
    c_ffn_dve = P.n("sDVE")
    if DEBUG:
        P.wait("sDVE", SY, c_ffn_dve)
        P.inc("sdX", SY.dma_start(dbg_ya.ap(), yacc[:, :]), 16)

    if ph == 6:
        return finish_early()
    for j in range(TPC):
        P.wait("sDVE", V, c_ffn_dve)
        P.wait("sDVE", A, c_ffn_dve)
        i_norm = emit_ln(yacc[:, j * D:(j + 1) * D], tmp2[:, :], dump=(0, 1))
        P.vt(i_norm)
        i = V.tensor_tensor(resid.ap()[:, j * D:(j + 1) * D],
                            resid.ap()[:, j * D:(j + 1) * D], tmp2[:, :], ALU.add)
        P.inc("sDVE", i)
    P.wait("sDVE", SY)
    for j in range(TPC):
        i = SY.dma_start(out_h.ap()[j * 128:(j + 1) * 128, :], resid[:, j * D:(j + 1) * D])
        P.inc("sdX", i, 16)
    SY.wait_ge(P.sem["sdX"][0], P.n("sdX"))

    stack.close()
    return nc



def _sb_chunked(w, kc_rows=128):
    R, C = w.shape
    k = R // kc_rows
    return np.ascontiguousarray(w.reshape(k, kc_rows, C).transpose(1, 0, 2).reshape(kc_rows, k * C))


def _prepare(L0_attn_hidden_states, L0_attn_Wq, L0_attn_Wk, L0_attn_Wv, L0_attn_Wo,
             L0_attn_bo, L0_ffn_w_router, L0_ffn_w_gate, L0_ffn_w_up, L0_ffn_w_down,
             L0_ffn_ws_gate, L0_ffn_ws_up, L0_ffn_ws_down, L0_ffn_w_shared_gate,
             ph=7):
    x = np.ascontiguousarray(np.asarray(L0_attn_hidden_states, np.float32).reshape(S, D))
    wq = np.asarray(L0_attn_Wq, np.float32)
    wk = np.asarray(L0_attn_Wk, np.float32)
    wv = np.asarray(L0_attn_Wv, np.float32)
    wo = np.asarray(L0_attn_Wo, np.float32)
    bo = np.asarray(L0_attn_bo, np.float32)
    wr = np.asarray(L0_ffn_w_router, np.float32)
    wg = np.asarray(L0_ffn_w_gate, np.float32)
    wu = np.asarray(L0_ffn_w_up, np.float32)
    wd = np.asarray(L0_ffn_w_down, np.float32)
    wsg = np.asarray(L0_ffn_ws_gate, np.float32)
    wsu = np.asarray(L0_ffn_ws_up, np.float32)
    wsd = np.asarray(L0_ffn_ws_down, np.float32)
    wshg = np.asarray(L0_ffn_w_shared_gate, np.float32)

    con = build_constants()

    wq_s = np.stack([_sb_chunked(wq[:, n * 512:(n + 1) * 512]) for n in range(4)])
    wk_s = _sb_chunked(wk)
    wv_s = _sb_chunked(wv)
    wo_s = _sb_chunked(wo)
    wr_s = _sb_chunked(wr)
    wshg_s = _sb_chunked(wshg)
    wg_s = np.stack([[_sb_chunked(wg[e][:, m * 128:(m + 1) * 128]) for m in range(11)]
                     for e in range(NE)])
    wu_s = np.stack([[_sb_chunked(wu[e][:, m * 128:(m + 1) * 128]) for m in range(11)]
                     for e in range(NE)])
    wd_s = np.ascontiguousarray(
        wd.reshape(NE, 11, 128, 4, 512).transpose(0, 3, 1, 2, 4))
    wsg_s = np.stack([_sb_chunked(wsg[:, mg * 128:(mg + 1) * 128]) for mg in range(44)])
    wsu_s = np.stack([_sb_chunked(wsu[:, mg * 128:(mg + 1) * 128]) for mg in range(44)])
    wsd_s = np.ascontiguousarray(
        wsd.reshape(44, 128, 4, 512).transpose(2, 0, 1, 3))
    bo_b = np.ascontiguousarray(np.broadcast_to(bo[None, :], (128, D)))

    nc = build_nc(ph)

    shared = dict(x=x, wq_s=wq_s, wk_s=wk_s, wv_s=wv_s, wo_s=wo_s, bo_b=bo_b,
                  wr_s=wr_s, wg_s=wg_s, wu_s=wu_s, wd_s=wd_s, wsg_s=wsg_s,
                  wsu_s=wsu_s, wsd_s=wsd_s, wshg_s=wshg_s,
                  cosn=con["cosn"], sinn=con["sinn"], augk=con["augk"],
                  mask0=con["mask0"], idn=con["idn"],
                  cst=np.ascontiguousarray(np.broadcast_to(
                      np.array([1.0 / 240.0, 30.0, -30.0, 1e-5, -1.0, 0, 0, 0],
                               np.float32)[None, :], (128, 8))))

    in_maps = []
    for c in range(NCORE):
        tiles = core_tiles(c)
        rows = np.concatenate([np.arange(t * 128, (t + 1) * 128) for t in tiles])
        m = dict(shared)
        m["x_own"] = np.ascontiguousarray(x[rows])
        m["cos_own"] = np.ascontiguousarray(con["cosn"][rows])
        m["sin_own"] = np.ascontiguousarray(con["sinn"][rows])
        m["augq"] = build_augq(c, con)
        in_maps.append(m)
    return nc, in_maps


def kernel(**inputs):
    nc, in_maps = _prepare(**inputs)
    res = run_bass_kernel_spmd(nc, in_maps, core_ids=list(range(NCORE)))

    out = np.empty((S, D), np.float32)
    for c in range(NCORE):
        shard = np.asarray(res.results[c]["out_shard"], np.float32)
        for j, t in enumerate(core_tiles(c)):
            out[t * 128:(t + 1) * 128] = shard[j * 128:(j + 1) * 128]
    return out.reshape(1, S, D)
